# version 04
# baseline (speedup 1.0000x reference)
import contextlib
import os
import numpy as np
import ml_dtypes
import concourse.bass as bass
import concourse.mybir as mybir
from concourse.bass_utils import run_bass_kernel_spmd

F32 = mybir.dt.float32
BF16 = mybir.dt.bfloat16
AF = mybir.ActivationFunctionType
ALU = mybir.AluOpType
AX = mybir.AxisListType

T = 4096
D = 4096
H = 4
HD = 1024
NB = 32
OB = 16
OT = OB * 128
EPS1 = 1e-5
EPS2 = 1.1920929e-07
ENGS = ("sync", "tensor", "vector", "scalar", "gpsimd")

PACKS = [[0]] + [[3 * q - 2, 3 * q - 1, 3 * q] for q in range(1, 6)]


def n_sb_of(pack):
    return 2 * pack[-1] + 2


def n_av_of(o):
    return 2 * o + 2


def build_program():
    SKIP = set(os.environ.get("KSKIP", "").split(","))
    NH = int(os.environ.get("KP2H", "4"))
    NPK = int(os.environ.get("KP2P", "6"))
    NOAV = "noav" in SKIP
    NOSIG = "nosig" in SKIP
    NOMASK = "nomask" in SKIP
    NOSC = "nosc" in SKIP
    NOPOOL = True
    KDBG = "1" == os.environ.get("KDBG", "")
    nc = bass.Bass("TRN2", target_bir_lowering=False)

    xfull = nc.dram_tensor("xfull", [T, D], F32, kind="ExternalInput").ap()
    xq = nc.dram_tensor("xq", [OT, D], F32, kind="ExternalInput").ap()
    mask1 = nc.dram_tensor("mask1", [2, 128, 128], F32, kind="ExternalInput").ap()
    mask3 = nc.dram_tensor("mask3", [6, 128, 384], F32, kind="ExternalInput").ap()
    bvq = nc.dram_tensor("bvq", [4, HD], F32, kind="ExternalInput").ap()
    bvk = nc.dram_tensor("bvk", [4, HD], F32, kind="ExternalInput").ap()
    bvv = nc.dram_tensor("bvv", [4, HD], F32, kind="ExternalInput").ap()
    wcat = nc.dram_tensor("wcat", [24, D], F32, kind="ExternalInput").ap()
    basecat = nc.dram_tensor("basecat", [1, 24], F32, kind="ExternalInput").ap()
    gate = nc.dram_tensor("gate", [1, 1], F32, kind="ExternalInput").ap()
    wbr = nc.dram_tensor("wbr", [1024, 1024], F32, kind="ExternalInput").ap()
    out = nc.dram_tensor("out", [OT, D], F32, kind="ExternalOutput").ap()

    xn_s = nc.dram_tensor("xn_s", [T, D], BF16, kind="Internal").ap()
    xnq_s = nc.dram_tensor("xnq_s", [OT, D], BF16, kind="Internal").ap()
    qfm_s = nc.dram_tensor("qfm_s", [32, OB, 128, 128], BF16, kind="Internal").ap()
    y_s = nc.dram_tensor("y_s", [OT, 1024], F32, kind="Internal").ap()
    s_s = nc.dram_tensor("s_s", [OB, 128, H, 1024], F32, kind="Internal").ap()
    sgnqk_s = nc.dram_tensor("sgnqk_s", [4, HD], BF16, kind="Internal").ap()
    sgnv_s = nc.dram_tensor("sgnv_s", [4, HD], BF16, kind="Internal").ap()
    asc_s = nc.dram_tensor("asc_s", [4, 1], F32, kind="Internal").ap()
    avsc_s = nc.dram_tensor("avsc_s", [4, 1], F32, kind="Internal").ap()
    g_s = nc.dram_tensor("g_s", [1, 2], F32, kind="Internal").ap()
    KDBG_ = "1" == os.environ.get("KDBG", "")
    if KDBG_:
        dbg_xn = nc.dram_tensor("dbg_xn", [T, D], BF16, kind="ExternalOutput").ap()
        dbg_xnq = nc.dram_tensor("dbg_xnq", [OT, D], BF16, kind="ExternalOutput").ap()
        dbg_z = nc.dram_tensor("dbg_z", [128, OB * 24], F32, kind="ExternalOutput").ap()
        dbg_h = nc.dram_tensor("dbg_h", [128, OB * 24], F32, kind="ExternalOutput").ap()
        dbg_y = nc.dram_tensor("dbg_y", [OT, 1024], F32, kind="ExternalOutput").ap()
        dbg_ss = nc.dram_tensor("dbg_ss", [OB, 128, H, 1024], F32, kind="ExternalOutput").ap()
        dbg_q = nc.dram_tensor("dbg_q", [32, OB, 128, 128], BF16, kind="ExternalOutput").ap()
        dbg_g = nc.dram_tensor("dbg_g", [OT, D], BF16, kind="ExternalOutput").ap()

    class SemC:
        def __init__(self, h):
            self.h = h
            self.n = 0

    prog = {e: [] for e in ENGS}
    pending = {e: [] for e in ENGS}

    def emit(eng, fn, waits=(), inc=None, k=1):
        w = pending[eng] + [x for x in waits if x is not None and x[1] > 0]
        pending[eng] = []
        tgt = None
        if inc is not None:
            inc.n += k
            tgt = (inc.h, k)
        prog[eng].append((w, fn, tgt))
        return inc.n if inc is not None else None

    with contextlib.ExitStack() as glob:
        GE = glob.enter_context
        sPE = SemC(GE(nc.semaphore("sPE")))
        sDV = SemC(GE(nc.semaphore("sDV")))
        sAC = SemC(GE(nc.semaphore("sAC")))
        sPO = SemC(GE(nc.semaphore("sPO")))
        ALL = (sPE, sDV, sAC, sPO)

        class DmaQ:
            def __init__(self, name, eng, R=6):
                self.eng = eng
                self.R = R
                self.slots = [SemC(GE(nc.semaphore(f"{name}{i}")))
                              for i in range(R)]
                self.count = 0

        qS = DmaQ("qS", "sync")
        qG = DmaQ("qG", "gpsimd")

        def barrier():
            snap = [(s.h, s.n) for s in ALL if s.n > 0]
            snap += [(s.h, s.n) for q in (qS, qG) for s in q.slots if s.n > 0]
            for e in ENGS:
                pending[e] = pending[e] + snap

        def dma(eng, o_, i_, waits=()):
            q = qS if eng == "sync" else qG
            slot = q.slots[q.count % q.R]
            q.count += 1
            prevfull = slot.n

            def fn(e, o_=o_, i_=i_):
                return e.dma_start(out=o_, in_=i_)
            emit(q.eng, fn, list(waits) + [(slot.h, prevfull)], inc=slot, k=16)
            return (slot.h, slot.n)

        pb = [GE(nc.psum_tensor(f"pb{i}", [128, 512], F32)) for i in range(8)]
        pbb = [p.ap().bitcast(BF16) for p in pb]

        ident = GE(nc.sbuf_tensor("ident", [128, 128], BF16))
        m1 = GE(nc.sbuf_tensor("m1", [128, 2 * 128], BF16))
        m3 = GE(nc.sbuf_tensor("m3", [128, 6 * 384], BF16))
        wcat_fm = GE(nc.sbuf_tensor("wcat_fm", [128, 32 * 24], BF16))
        base10 = GE(nc.sbuf_tensor("base10", [128, 16], F32))
        baseA = GE(nc.sbuf_tensor("baseA", [128, 24], F32))
        sigsc = GE(nc.sbuf_tensor("sigsc", [128, 4], F32))
        avsc = GE(nc.sbuf_tensor("avsc", [128, 4], F32))
        gomg = GE(nc.sbuf_tensor("gomg", [128, 2], F32))
        Z = GE(nc.sbuf_tensor("Z", [128, OB * 24], F32))
        Hres = GE(nc.sbuf_tensor("Hres", [128, OB * 16], F32))
        Hpre = GE(nc.sbuf_tensor("Hpre", [128, OB * 4], F32))
        Hpost = GE(nc.sbuf_tensor("Hpost", [128, OB * 4], F32))
        nu = GE(nc.sbuf_tensor("nu", [128, 64], F32))
        nv = GE(nc.sbuf_tensor("nv", [128, 64], F32))
        tsk = GE(nc.sbuf_tensor("tsk", [128, 256], F32))
        epst = GE(nc.sbuf_tensor("epst", [128, 2], F32))
        warm = GE(nc.sbuf_tensor("warm", [128, 2], F32))
        esk = GE(nc.sbuf_tensor("esk", [128, 256], F32))
        ssk = GE(nc.sbuf_tensor("ssk", [128, 64], F32))

        with contextlib.ExitStack() as p0:
            PE0 = p0.enter_context
            bq = PE0(nc.sbuf_tensor("bq", [4, HD], F32))
            bk = PE0(nc.sbuf_tensor("bk", [4, HD], F32))
            bv = PE0(nc.sbuf_tensor("bv", [4, HD], F32))
            tq = PE0(nc.sbuf_tensor("tq", [4, HD], F32))
            tk = PE0(nc.sbuf_tensor("tk", [4, HD], F32))
            tv_ = PE0(nc.sbuf_tensor("tv_", [4, HD], F32))
            al4 = PE0(nc.sbuf_tensor("al4", [4, 4], F32))
            gt = PE0(nc.sbuf_tensor("gt", [1, 2], F32))
            wc32 = PE0(nc.sbuf_tensor("wc32", [24, D], F32))
            wcb = PE0(nc.sbuf_tensor("wcb", [24, D], BF16))

            m_bq = dma("sync", bq[:, :], bvq[:, :])
            m_bk = dma("sync", bk[:, :], bvk[:, :])
            m_bv = dma("sync", bv[:, :], bvv[:, :])
            m_wc = dma("sync", wc32[:, :], wcat[:, :])
            m_gt = dma("sync", gt[:, 0:1], gate[:, :])
            dma("gpsimd", m1[:, :].rearrange("p (k t) -> p k t", k=2),
                mask1[:, :, :].rearrange("k s t -> s k t"))
            dma("gpsimd", m3[:, :].rearrange("p (k t) -> p k t", k=6),
                mask3[:, :, :].rearrange("k s t -> s k t"))
            m_ba = dma("sync", baseA[:, :], basecat[0:1, :].broadcast_to((128, 24)))

            emit("gpsimd", lambda e: e.memset(ident[:, :], 0.0))
            emit("vector", lambda e: e.memset(epst[:, 0:1], EPS1), inc=sDV)
            emit("vector", lambda e: e.memset(epst[:, 1:2], EPS2), inc=sDV)
            emit("gpsimd", lambda e: e.affine_select(
                out=ident[:, :], in_=ident[:, :], compare_op=ALU.not_equal,
                fill=1.0, base=0, pattern=[[-1, 128]], channel_multiplier=1),
                inc=sPO)
            mk_id = sPO.n

            emit("vector", lambda e: e.tensor_scalar(
                out=base10[:, :], in0=baseA[:, 0:16], scalar1=10.0, scalar2=None,
                op0=ALU.mult), waits=[m_ba], inc=sDV)

            for src, col, wm in ((bq, 0, m_bq), (bk, 1, m_bk), (bv, 2, m_bv)):
                emit("vector", lambda e, src=src, col=col: e.reduce_sum(
                    out=al4[:, col:col + 1], in_=src[:, :], axis=AX.X,
                    apply_absolute_value=True), waits=[wm], inc=sDV)
            mk_alraw = sDV.n
            emit("vector", lambda e: e.tensor_mul(
                al4[:, 3:4], al4[:, 0:1], al4[:, 1:2]),
                waits=[(sDV.h, mk_alraw)], inc=sDV)
            mk_t1 = sDV.n
            emit("vector", lambda e: e.tensor_scalar(
                out=al4[:, 3:4], in0=al4[:, 3:4],
                scalar1=0.125 / (1024.0 * 1024.0), scalar2=None, op0=ALU.mult),
                waits=[(sDV.h, mk_t1)], inc=sDV)
            emit("vector", lambda e: e.tensor_scalar(
                out=al4[:, 2:3], in0=al4[:, 2:3], scalar1=1.0 / 1024.0,
                scalar2=None, op0=ALU.mult), waits=[(sDV.h, mk_alraw)], inc=sDV)
            mk_al = sDV.n
            emit("scalar", lambda e: e.activation(
                gt[:, 0:1], gt[:, 0:1], AF.Sigmoid), waits=[m_gt], inc=sAC)
            emit("vector", lambda e: e.tensor_scalar(
                out=gt[:, 1:2], in0=gt[:, 0:1], scalar1=-1.0, scalar2=1.0,
                op0=ALU.mult, op1=ALU.add), waits=[(sAC.h, sAC.n)], inc=sDV)
            mk_gt = sDV.n

            pending["scalar"] = pending["scalar"] + [(sDV.h, mk_alraw)]
            for src, tmp in ((bq, tq), (bk, tk), (bv, tv_)):
                emit("scalar", lambda e, src=src: e.activation(
                    src[:, :], src[:, :], AF.Sign), inc=sAC)
                emit("scalar", lambda e, src=src, tmp=tmp: e.activation(
                    tmp[:, :], src[:, :], AF.Square), inc=sAC)
                mk_sq2 = sAC.n
                emit("vector", lambda e, src=src, tmp=tmp: e.tensor_sub(
                    src[:, :], src[:, :], tmp[:, :]), waits=[(sAC.h, mk_sq2)],
                    inc=sDV)
                emit("vector", lambda e, src=src: e.tensor_scalar(
                    out=src[:, :], in0=src[:, :], scalar1=1.0, scalar2=None,
                    op0=ALU.add), inc=sDV)
            emit("vector", lambda e: e.tensor_mul(bq[:, :], bq[:, :], bk[:, :]),
                 inc=sDV)
            mk_sgn = sDV.n

            m_sq = dma("gpsimd", sgnqk_s[:, :], bq[:, :], waits=[(sDV.h, mk_sgn)])
            m_sv = dma("gpsimd", sgnv_s[:, :], bv[:, :], waits=[(sDV.h, mk_sgn)])
            m_as = dma("gpsimd", asc_s[:, :], al4[:, 3:4], waits=[(sDV.h, mk_al)])
            m_av = dma("gpsimd", avsc_s[:, :], al4[:, 2:3], waits=[(sDV.h, mk_al)])
            m_gs = dma("gpsimd", g_s[0:1, :], gt[0:1, :], waits=[(sDV.h, mk_gt)])
            dma("sync", sigsc[:, :],
                asc_s[:, :].rearrange("a b -> b a").broadcast_to((128, 4)),
                waits=[m_as])
            dma("sync", avsc[:, :],
                avsc_s[:, :].rearrange("a b -> b a").broadcast_to((128, 4)),
                waits=[m_av])
            dma("sync", gomg[:, :], g_s[0:1, :].broadcast_to((128, 2)),
                waits=[m_gs])

            emit("vector", lambda e: e.tensor_copy(out=wcb[:, :], in_=wc32[:, :]),
                 waits=[m_wc], inc=sDV)
            mk_wcb = sDV.n
            for half in range(2):
                def tr16(e, half=half):
                    i = None
                    for c in range(16):
                        cc = half * 16 + c
                        i = e.matmul(pb[half][:, c * 24:(c + 1) * 24],
                                     wcb[0:24, cc * 128:(cc + 1) * 128],
                                     ident[0:24, 0:24],
                                     start=True, stop=True,
                                     skip_group_check=True)
                    return i
                emit("tensor", tr16, waits=[(sDV.h, mk_wcb), (sPO.h, mk_id)],
                     inc=sPE)
                mk_t = sPE.n
                emit("scalar", lambda e, half=half: e.activation(
                    wcat_fm[:, half * 384:(half + 1) * 384],
                    pb[half][:, 0:384], AF.Identity),
                    waits=[(sPE.h, mk_t)], inc=sAC)
        barrier()

        with contextlib.ExitStack() as p1:
            emit("scalar", lambda e: e.activation(
                warm[:, 0:1], epst[:, 0:1], AF.Ln), inc=sAC)
            emit("scalar", lambda e: e.activation(
                warm[:, 0:1], epst[:, 0:1], AF.Exp), inc=sAC)
            PE1 = p1.enter_context
            xst = [PE1(nc.sbuf_tensor(f"xst{i}", [128, D], F32)) for i in range(2)]
            xnt = [PE1(nc.sbuf_tensor(f"xnt{i}", [128, D], BF16)) for i in range(2)]
            st6 = [PE1(nc.sbuf_tensor(f"st6_{i}", [128, 48], F32)) for i in range(2)]
            stv = [PE1(nc.sbuf_tensor(f"stv{i}", [128, 8], F32)) for i in range(2)]

            mark_apply = {}
            mark_store = {}
            for i in range(0 if 'p1' in SKIP else NB):
                k = i % 2
                mld = dma("sync", xst[k][:, :],
                          xfull[i * 128:(i + 1) * 128, :],
                          waits=[(sAC.h, mark_apply.get(i - 2, 0))])
                def bns(e, k=k):
                    i_ = None
                    for j in range(8):
                        i_ = e.bn_stats(out=st6[k][:, j * 6:(j + 1) * 6],
                                        in_=xst[k][:, j * 512:(j + 1) * 512])
                    return i_
                emit("vector", bns, waits=[mld], inc=sDV)
                mk_bns = sDV.n
                emit("vector", lambda e, k=k: e.bn_aggr(
                    out=stv[k][:, 0:2], in_=st6[k][:, :]),
                    waits=[(sDV.h, mk_bns)], inc=sDV)
                mk_bn = sDV.n
                emit("scalar", lambda e, k=k: e.activation(
                    stv[k][:, 2:3], stv[k][:, 1:2], AF.Ln, bias=epst[:, 0:1]),
                    waits=[(sDV.h, mk_bn)], inc=sAC)
                mk_ln = sAC.n
                emit("scalar", lambda e, k=k: e.activation(
                    stv[k][:, 2:3], stv[k][:, 2:3], AF.Exp, scale=-0.5),
                    waits=[(sAC.h, mk_ln)], inc=sAC)
                mk_rstd = sAC.n
                emit("vector", lambda e, k=k: e.tensor_scalar(
                    out=stv[k][:, 3:4], in0=stv[k][:, 0:1], scalar1=stv[k][:, 2:3],
                    scalar2=-1.0, op0=ALU.mult, op1=ALU.mult),
                    waits=[(sAC.h, mk_rstd), (sDV.h, mk_bn)], inc=sDV)
                mk_nm = sDV.n
                mark_apply[i] = emit("scalar", lambda e, k=k: e.activation(
                    xnt[k][:, :], xst[k][:, :], AF.Identity,
                    bias=stv[k][:, 3:4], scale=stv[k][:, 2:3]),
                    waits=[(sDV.h, mk_nm), (sAC.h, mk_rstd),
                           mark_store.get(i - 2)],
                    inc=sAC)
                mark_store[i] = dma("gpsimd",
                                    xn_s[i * 128:(i + 1) * 128, :],
                                    xnt[k][:, :],
                                    waits=[(sAC.h, mark_apply[i])])
        barrier()

        with contextlib.ExitStack() as p15:
            PE15 = p15.enter_context
            xqst = [PE15(nc.sbuf_tensor(f"xqst{i}", [128, D], F32)) for i in range(2)]
            xnt2 = [PE15(nc.sbuf_tensor(f"xnt2_{i}", [128, D], BF16)) for i in range(2)]
            xnfm = [PE15(nc.sbuf_tensor(f"xnfm{i}", [128, 32 * 128], BF16)) for i in range(2)]
            st6b = PE15(nc.sbuf_tensor("st6b", [128, 48], F32))
            stvb = [PE15(nc.sbuf_tensor(f"stvb{i}", [128, 8], F32)) for i in range(2)]
            wb_fm = PE15(nc.sbuf_tensor("wb_fm", [128, 8 * 1024], BF16))
            wb_st = PE15(nc.sbuf_tensor("wb_st", [128, 1024], F32))
            acc = [PE15(nc.sbuf_tensor(f"acc{i}", [128, 1024], F32)) for i in range(2)]
            xpb = [PE15(nc.sbuf_tensor(f"xpb{i}", [128, 1024], BF16)) for i in range(2)]
            xpfm = [PE15(nc.sbuf_tensor(f"xpfm{i}", [128, 1024], BF16)) for i in range(2)]
            yst = [PE15(nc.sbuf_tensor(f"yst{i}", [128, 1024], F32)) for i in range(2)]

            mk_wb = {}
            for c in range(8):
                ml = dma("sync", wb_st[:, :], wbr[c * 128:(c + 1) * 128, :],
                         waits=[(sDV.h, mk_wb.get(c - 1, 0))])
                mk_wb[c] = emit("vector", lambda e, c=c: e.tensor_copy(
                    out=wb_fm[:, c * 1024:(c + 1) * 1024], in_=wb_st[:, :]),
                    waits=[ml], inc=sDV)

            mk_apl = {}
            mk_xnqst = {}
            _OB_A = 0 if 'p15a' in SKIP else OB
            mk_qdma = {}
            mk_evac = {}
            mk_trlast = {}
            mk_logit_evac = {}
            for o in range(_OB_A):
                k = o % 2
                mld = dma("sync", xqst[k][:, :],
                          xq[o * 128:(o + 1) * 128, :],
                          waits=[(sAC.h, mk_apl.get(o - 2, 0))])
                def bns2(e, k=k):
                    i_ = None
                    for j in range(8):
                        i_ = e.bn_stats(out=st6b[:, j * 6:(j + 1) * 6],
                                        in_=xqst[k][:, j * 512:(j + 1) * 512])
                    return i_
                emit("vector", bns2, waits=[mld], inc=sDV)
                mk_bns = sDV.n
                emit("vector", lambda e, k=k: e.bn_aggr(
                    out=stvb[k][:, 0:2], in_=st6b[:, :]),
                    waits=[(sDV.h, mk_bns)], inc=sDV)
                mk_bn = sDV.n
                emit("scalar", lambda e, k=k: e.activation(
                    stvb[k][:, 2:3], stvb[k][:, 1:2], AF.Ln, bias=epst[:, 0:1]),
                    waits=[(sDV.h, mk_bn)], inc=sAC)
                mk_ln1 = sAC.n
                emit("scalar", lambda e, k=k: e.activation(
                    stvb[k][:, 2:3], stvb[k][:, 2:3], AF.Exp, scale=-0.5),
                    waits=[(sAC.h, mk_ln1)], inc=sAC)
                mk_rstd = sAC.n
                emit("vector", lambda e, k=k: e.tensor_scalar(
                    out=stvb[k][:, 3:4], in0=stvb[k][:, 0:1], scalar1=stvb[k][:, 2:3],
                    scalar2=-1.0, op0=ALU.mult, op1=ALU.mult),
                    waits=[(sAC.h, mk_rstd), (sDV.h, mk_bn)], inc=sDV)
                emit("vector", lambda e, k=k: e.tensor_mul(
                    stvb[k][:, 6:7], stvb[k][:, 2:3], stvb[k][:, 2:3]),
                    waits=[(sAC.h, mk_rstd)], inc=sDV)
                mk_t6 = sDV.n
                emit("vector", lambda e, k=k: e.tensor_mul(
                    stvb[k][:, 6:7], stvb[k][:, 6:7], stvb[k][:, 1:2]),
                    waits=[(sDV.h, mk_t6)], inc=sDV)
                mk_q = sDV.n
                emit("scalar", lambda e, k=k: e.activation(
                    stvb[k][:, 4:5], stvb[k][:, 6:7], AF.Ln, bias=epst[:, 1:2]),
                    waits=[(sDV.h, mk_q)], inc=sAC)
                mk_ln2 = sAC.n
                emit("scalar", lambda e, k=k: e.activation(
                    stvb[k][:, 4:5], stvb[k][:, 4:5], AF.Exp, scale=-0.5),
                    waits=[(sAC.h, mk_ln2)], inc=sAC)
                mk_r2 = sAC.n
                emit("vector", lambda e, k=k: e.tensor_scalar(
                    out=stvb[k][:, 5:6], in0=stvb[k][:, 4:5], scalar1=10.0,
                    scalar2=None, op0=ALU.mult), waits=[(sAC.h, mk_r2)], inc=sDV)
                mk_r10 = sDV.n
                mk_apl[o] = emit("scalar", lambda e, k=k: e.activation(
                    xnt2[k][:, :], xqst[k][:, :], AF.Identity,
                    bias=stvb[k][:, 3:4], scale=stvb[k][:, 2:3]),
                    waits=[mk_xnqst.get(o - 2),
                           (sPE.h, mk_trlast.get(o - 2, 0)),
                           (sAC.h, mk_r2), (sDV.h, mk_r10)], inc=sAC)
                mk_xnqst[o] = dma("gpsimd",
                                  xnq_s[o * 128:(o + 1) * 128, :],
                                  xnt2[k][:, :],
                                  waits=[(sAC.h, mk_apl[o])])
                for cg in range(8):
                    bank = cg % 2
                    def trg(e, k=k, cg=cg, bank=bank):
                        i = None
                        for c in range(4):
                            cc = cg * 4 + c
                            i = e.matmul(pb[bank][:, c * 128:(c + 1) * 128],
                                         xnt2[k][:, cc * 128:(cc + 1) * 128],
                                         ident[:, :], start=True, stop=True,
                                         skip_group_check=True)
                        return i
                    trw = [(sAC.h, mk_apl[o]), (sAC.h, mk_evac.get((o, cg - 2), 0))]
                    if cg < 2:
                        trw.append((sAC.h, mk_evac.get((o - 1, cg + 6), 0)))
                    emit("tensor", trg, waits=trw, inc=sPE)
                    mk_tr = sPE.n
                    mk_evac[(o, cg)] = emit(
                        "scalar", lambda e, k=k, cg=cg, bank=bank: e.activation(
                            xnfm[k][:, cg * 512:(cg + 1) * 512],
                            pb[bank][:, 0:512], AF.Identity),
                        waits=[(sPE.h, mk_tr), mk_qdma.get(o - 2)],
                        inc=sAC)
                    def lgt(e, k=k, cg=cg):
                        i = None
                        for c in range(4):
                            cc = cg * 4 + c
                            i = e.matmul(pb[2][:, 0:24],
                                         xnfm[k][:, cc * 128:(cc + 1) * 128],
                                         wcat_fm[:, cc * 24:(cc + 1) * 24],
                                         start=(cc == 0), stop=(cc == 31),
                                         skip_group_check=True)
                        return i
                    lw = [(sAC.h, mk_evac[(o, cg)])]
                    if cg == 0:
                        lw.append((sDV.h, mk_logit_evac.get(o - 1, 0)))
                    emit("tensor", lgt, waits=lw, inc=sPE)
                mk_trlast[o] = sPE.n
                mk_lg = sPE.n
                mk_qdma[o] = dma("gpsimd",
                                 qfm_s[:, o, :, :].rearrange("c d t -> d c t"),
                                 xnfm[k][:, :].rearrange("d (c t) -> d c t", c=32),
                                 waits=[(sAC.h, mk_evac[(o, 7)])])
                emit("vector", lambda e, o=o, k=k: e.scalar_tensor_tensor(
                    out=Z[:, o * 24:o * 24 + 16], in0=pb[2][:, 0:16],
                    scalar=stvb[k][:, 5:6], in1=base10[:, :],
                    op0=ALU.mult, op1=ALU.add),
                    waits=[(sPE.h, mk_lg), (sDV.h, mk_r10)], inc=sDV)
                mk_logit_evac[o] = emit("vector", lambda e, o=o, k=k: e.scalar_tensor_tensor(
                    out=Z[:, o * 24 + 16:o * 24 + 24], in0=pb[2][:, 16:24],
                    scalar=stvb[k][:, 4:5], in1=baseA[:, 16:24],
                    op0=ALU.mult, op1=ALU.add), inc=sDV)
            barrier()

            def zview(ch, r_=None, c_=None):
                v = Z[:, ch * 192:(ch + 1) * 192].rearrange(
                    "p (o i) -> p o i", i=24)[:, :, 0:16].rearrange(
                    "p o (r c) -> p o r c", c=4)
                if r_ is not None:
                    v = v[:, :, r_:r_ + 1, :].squeeze(2)
                if c_ is not None:
                    v = v[:, :, :, c_:c_ + 1].squeeze(3)
                return v

            def chviews(ch):
                tvv = tsk[:, ch * 128:(ch + 1) * 128].rearrange(
                    "p (o r c) -> p o r c", r=4, c=4)
                evv = esk[:, ch * 128:(ch + 1) * 128].rearrange(
                    "p (o r c) -> p o r c", r=4, c=4)
                nuv = nu[:, ch * 32:(ch + 1) * 32].rearrange("p (o r) -> p o r", r=4)
                nvv = nv[:, ch * 32:(ch + 1) * 32].rearrange("p (o c) -> p o c", c=4)
                return tvv, evv, nuv, nvv

            emit("vector", lambda e: e.memset(nv[:, :], 0.0), inc=sDV)
            ac_prev = {0: 0, 1: 0}
            dv_prev = {0: sDV.n, 1: sDV.n}
            for it in range(0 if 'p15b' in SKIP else 50):
                for ch in range(2):
                    tvv, evv, nuv, nvv = chviews(ch)
                    for r_ in range(4):
                        emit("vector", lambda e, ch=ch, r_=r_, tvv=tvv, nvv=nvv:
                             e.tensor_tensor(out=tvv[:, :, r_:r_ + 1, :].squeeze(2),
                                             in0=zview(ch, r_=r_), in1=nvv[:, :, :],
                                             op=ALU.subtract),
                             waits=[(sAC.h, ac_prev[ch])], inc=sDV)
                    mk_t = sDV.n
                    emit("scalar", lambda e, tvv=tvv, evv=evv: e.activation(
                        evv[:, :, :, :], tvv[:, :, :, :], AF.Exp),
                        waits=[(sDV.h, mk_t)], inc=sAC)
                    mk_e = sAC.n
                    emit("vector", lambda e, ch=ch, evv=evv: e.reduce_sum(
                        out=ssk[:, ch * 32:(ch + 1) * 32], in_=evv[:, :, :, :],
                        axis=AX.X), waits=[(sAC.h, mk_e)], inc=sDV)
                    mk_s = sDV.n
                    emit("scalar", lambda e, ch=ch: e.activation(
                        nu[:, ch * 32:(ch + 1) * 32],
                        ssk[:, ch * 32:(ch + 1) * 32], AF.Ln),
                        waits=[(sDV.h, mk_s)], inc=sAC)
                    ac_prev[ch] = sAC.n
                for ch in range(2):
                    tvv, evv, nuv, nvv = chviews(ch)
                    for c_ in range(4):
                        emit("vector", lambda e, ch=ch, c_=c_, tvv=tvv, nuv=nuv:
                             e.tensor_tensor(
                                 out=tvv[:, :, :, c_:c_ + 1].squeeze(3),
                                 in0=zview(ch, c_=c_), in1=nuv[:, :, :],
                                 op=ALU.subtract),
                             waits=[(sAC.h, ac_prev[ch])], inc=sDV)
                    mk_t2 = sDV.n
                    emit("scalar", lambda e, tvv=tvv, evv=evv: e.activation(
                        evv[:, :, :, :], tvv[:, :, :, :], AF.Exp),
                        waits=[(sDV.h, mk_t2)], inc=sAC)
                    mk_e2 = sAC.n
                    emit("vector", lambda e, ch=ch, evv=evv: e.reduce_sum(
                        out=ssk[:, ch * 32:(ch + 1) * 32],
                        in_=evv[:, :, :, :].transpose([0, 1, 3, 2]), axis=AX.X),
                        waits=[(sAC.h, mk_e2)], inc=sDV)
                    mk_s2 = sDV.n
                    emit("scalar", lambda e, ch=ch: e.activation(
                        nv[:, ch * 32:(ch + 1) * 32],
                        ssk[:, ch * 32:(ch + 1) * 32], AF.Ln),
                        waits=[(sDV.h, mk_s2)], inc=sAC)
                    ac_prev[ch] = sAC.n
            for ch in range(2):
                tvv, evv, nuv, nvv = chviews(ch)
                hv = Hres[:, ch * 128:(ch + 1) * 128].rearrange(
                    "p (o r c) -> p o r c", r=4, c=4)
                for c_ in range(4):
                    emit("vector", lambda e, ch=ch, c_=c_, tvv=tvv, nuv=nuv:
                         e.tensor_tensor(out=tvv[:, :, :, c_:c_ + 1].squeeze(3),
                                         in0=zview(ch, c_=c_), in1=nuv[:, :, :],
                                         op=ALU.subtract),
                         waits=[(sAC.h, ac_prev[ch])], inc=sDV)
                mk_g1 = sDV.n
                for r_ in range(4):
                    emit("vector", lambda e, r_=r_, tvv=tvv, nvv=nvv:
                         e.tensor_tensor(out=tvv[:, :, r_:r_ + 1, :].squeeze(2),
                                         in0=tvv[:, :, r_:r_ + 1, :].squeeze(2),
                                         in1=nvv[:, :, :], op=ALU.subtract),
                         waits=[(sDV.h, mk_g1)], inc=sDV)
                mk_t = sDV.n
                emit("scalar", lambda e, hv=hv, tvv=tvv: e.activation(
                    hv[:, :, :, :], tvv[:, :, :, :], AF.Exp),
                    waits=[(sDV.h, mk_t)], inc=sAC)
                mk_h = sAC.n
                emit("vector", lambda e, ch=ch: e.tensor_scalar(
                    out=Hres[:, ch * 128:(ch + 1) * 128],
                    in0=Hres[:, ch * 128:(ch + 1) * 128],
                    scalar1=gomg[:, 0:1], scalar2=None, op0=ALU.mult),
                    waits=[(sAC.h, mk_h)], inc=sDV)
                mk_hg = sDV.n
                emit("vector", lambda e, ch=ch: e.tensor_scalar(
                    out=Hres[:, ch * 128:(ch + 1) * 128].rearrange(
                        "p (o i) -> p o i", i=16)[:, :, 0:16:5],
                    in0=Hres[:, ch * 128:(ch + 1) * 128].rearrange(
                        "p (o i) -> p o i", i=16)[:, :, 0:16:5],
                    scalar1=gomg[:, 1:2], scalar2=None, op0=ALU.add),
                    waits=[(sDV.h, mk_hg)], inc=sDV)
            zpre = Z[:, :].rearrange("p (o i) -> p o i", i=24)[:, :, 16:20]
            zpost = Z[:, :].rearrange("p (o i) -> p o i", i=24)[:, :, 20:24]
            for zsrc, hdst in ((zpre, Hpre), (zpost, Hpost)):
                emit("scalar", lambda e, zsrc=zsrc, hdst=hdst: e.activation(
                    hdst[:, :].rearrange("p (o c) -> p o c", c=4), zsrc, AF.Exp),
                    inc=sAC)
                mk_e = sAC.n
                emit("vector", lambda e, hdst=hdst: e.reduce_sum(
                    out=ssk[:, 0:16],
                    in_=hdst[:, :].rearrange("p (o c) -> p o c", c=4), axis=AX.X),
                    waits=[(sAC.h, mk_e)], inc=sDV)
                mk_rs = sDV.n
                emit("vector", lambda e: e.reciprocal(
                    out=ssk[:, 16:32], in_=ssk[:, 0:16]),
                    waits=[(sDV.h, mk_rs)], inc=sDV)
                mk_rc = sDV.n
                for c_ in range(4):
                    emit("vector", lambda e, hdst=hdst, c_=c_: e.tensor_tensor(
                        out=hdst[:, :].rearrange(
                            "p (o c) -> p o c", c=4)[:, :, c_:c_ + 1].squeeze(2),
                        in0=hdst[:, :].rearrange(
                            "p (o c) -> p o c", c=4)[:, :, c_:c_ + 1].squeeze(2),
                        in1=ssk[:, 16:32], op=ALU.mult),
                        waits=[(sDV.h, mk_rc)], inc=sDV)
            barrier()

            mk_xp = {}
            mk_trx = {}
            _OB_C = 0 if 'p15c' in SKIP else OB
            mk_evx = {}
            mk_ymm = {}
            mk_ye = {}
            mk_yst = {}
            for o in range(_OB_C):
                k = o % 2
                mld = dma("gpsimd", xnt2[k][:, :],
                          xnq_s[o * 128:(o + 1) * 128, :],
                          waits=[(sDV.h, mk_xp.get(o - 2, 0))])
                emit("vector", lambda e, o=o, k=k: e.tensor_scalar(
                    out=acc[0][:, :], in0=xnt2[k][:, 0:1024],
                    scalar1=Hpre[:, o * 4:o * 4 + 1], scalar2=None, op0=ALU.mult),
                    waits=[mld], inc=sDV)
                emit("vector", lambda e, o=o, k=k: e.scalar_tensor_tensor(
                    out=acc[1][:, :], in0=xnt2[k][:, 1024:2048],
                    scalar=Hpre[:, o * 4 + 1:o * 4 + 2], in1=acc[0][:, :],
                    op0=ALU.mult, op1=ALU.add), inc=sDV)
                emit("vector", lambda e, o=o, k=k: e.scalar_tensor_tensor(
                    out=acc[0][:, :], in0=xnt2[k][:, 2048:3072],
                    scalar=Hpre[:, o * 4 + 2:o * 4 + 3], in1=acc[1][:, :],
                    op0=ALU.mult, op1=ALU.add), inc=sDV)
                emit("vector", lambda e, o=o, k=k: e.scalar_tensor_tensor(
                    out=acc[1][:, :], in0=xnt2[k][:, 3072:4096],
                    scalar=Hpre[:, o * 4 + 3:o * 4 + 4], in1=acc[0][:, :],
                    op0=ALU.mult, op1=ALU.add), inc=sDV)
                emit("vector", lambda e, k=k: e.tensor_copy(
                    out=xpb[k][:, :], in_=acc[1][:, :]),
                    waits=[(sPE.h, mk_trx.get(o - 2, 0))], inc=sDV)
                mk_xp[o] = sDV.n
                for g4 in range(2):
                    def trxp(e, k=k, g4=g4):
                        i = None
                        for c in range(4):
                            cc = g4 * 4 + c
                            i = e.matmul(pb[g4][:, c * 128:(c + 1) * 128],
                                         xpb[k][:, cc * 128:(cc + 1) * 128],
                                         ident[:, :], start=True, stop=True,
                                         skip_group_check=True)
                        return i
                    emit("tensor", trxp,
                         waits=[(sDV.h, mk_xp[o]),
                                (sAC.h, mk_evx.get((o - 1, g4), 0))],
                         inc=sPE)
                    mk_trx[o] = sPE.n
                    mk_evx[(o, g4)] = emit("scalar", lambda e, k=k, g4=g4:
                                           e.activation(
                        xpfm[k][:, g4 * 512:(g4 + 1) * 512],
                        pb[g4][:, 0:512], AF.Identity),
                        waits=[(sPE.h, mk_trx[o]),
                               (sPE.h, mk_ymm.get(o - 2, 0))],
                        inc=sAC)
                mk_evx[o] = sAC.n
                def ymm(e, k=k):
                    i = None
                    for half in range(2):
                        for c in range(8):
                            i = e.matmul(pb[3 + half][:, :],
                                         xpfm[k][:, c * 128:(c + 1) * 128],
                                         wb_fm[:, c * 1024 + half * 512:
                                               c * 1024 + (half + 1) * 512],
                                         start=(c == 0), stop=(c == 7),
                                         skip_group_check=True)
                    return i
                emit("tensor", ymm,
                     waits=[(sAC.h, mk_evx[o]), (sDV.h, mk_ye.get(o - 1, 0)),
                            (sDV.h, mk_wb[7])], inc=sPE)
                mk_ymm[o] = sPE.n
                for half in range(2):
                    emit("vector", lambda e, k=k, half=half: e.tensor_copy(
                        out=yst[k][:, half * 512:(half + 1) * 512],
                        in_=pb[3 + half][:, :]),
                        waits=[(sPE.h, mk_ymm[o]), mk_yst.get(o - 2)],
                        inc=sDV)
                mk_ye[o] = sDV.n
                mk_yst[o] = dma("gpsimd", y_s[o * 128:(o + 1) * 128, :],
                                yst[k][:, :], waits=[(sDV.h, mk_ye[o])])
        barrier()

        with contextlib.ExitStack() as p2:
            PE2 = p2.enter_context
            K = PE2(nc.sbuf_tensor("K", [128, 32 * 1024], BF16))
            V = PE2(nc.sbuf_tensor("V", [128, 32 * 1024], BF16))
            Qp = [PE2(nc.sbuf_tensor(f"Qp{i}", [128, 8 * 384], BF16)) for i in range(2)]
            attnT = [PE2(nc.sbuf_tensor(f"attnT{i}", [128, 384], BF16)) for i in range(2)]
            xn4 = [PE2(nc.sbuf_tensor(f"xn4_{i}", [128, 4096], BF16)) for i in range(2)]
            xsg4 = [PE2(nc.sbuf_tensor(f"xsg4_{i}", [128, 4096], BF16)) for i in range(2)]
            sgn_h = PE2(nc.sbuf_tensor("sgn_h", [128, 1024], BF16))
            sgv_h = PE2(nc.sbuf_tensor("sgv_h", [128, 1024], BF16))
            sst = [PE2(nc.sbuf_tensor(f"sst{i}", [128, 1024], F32)) for i in range(2)]

            emit("scalar", lambda e: e.activation(
                warm[:, 0:1], epst[:, 0:1], AF.Sigmoid), inc=sAC)
            for h in range(0 if 'p2' in SKIP else NH):
                barrier()
                m_sg = dma("sync", sgn_h[:, :],
                           sgnqk_s[h:h + 1, :].broadcast_to((128, HD)))
                m_sv = dma("sync", sgv_h[:, :],
                           sgnv_s[h:h + 1, :].broadcast_to((128, HD)))

                mk_tt = {}
                mk_ktr = {}
                mk_kev = {}
                for gidx in range(8):
                    k = gidx % 2
                    mld = dma("sync",
                              xn4[k][:, :].rearrange("p (f d) -> p f d", f=4),
                              xn_s[gidx * 512:(gidx + 1) * 512,
                                   h * 1024:(h + 1) * 1024].rearrange(
                                       "(f p) d -> p f d", p=128),
                              waits=[m_sv, (sDV.h, mk_tt.get(gidx - 2, 0)),
                                     (sPE.h, mk_ktr.get((gidx - 2, 3), 0))])
                    def ttg(e, k=k):
                        i = None
                        for f in range(4):
                            i = e.tensor_tensor(
                                out=xsg4[k][:, f * 1024:(f + 1) * 1024],
                                in0=xn4[k][:, f * 1024:(f + 1) * 1024],
                                in1=sgn_h[:, :], op=ALU.mult)
                        return i
                    def ttv(e, gidx=gidx, k=k):
                        i = None
                        for f in range(4):
                            sb = gidx * 4 + f
                            i = e.tensor_tensor(
                                out=V[:, sb * 1024:(sb + 1) * 1024],
                                in0=xn4[k][:, f * 1024:(f + 1) * 1024],
                                in1=sgv_h[:, :], op=ALU.mult)
                        return i
                    emit("vector", ttg, waits=[mld, m_sg], inc=sDV)
                    emit("vector", ttv, waits=[m_sv], inc=sDV)
                    mk_tt[gidx] = sDV.n
                    for f in range(4):
                        sb = gidx * 4 + f
                        for g4 in range(2):
                            bank = (sb * 2 + g4) % 2
                            def ktr(e, k=k, f=f, g4=g4, bank=bank):
                                i = None
                                for c in range(4):
                                    cc = f * 8 + g4 * 4 + c
                                    i = e.matmul(
                                        pb[bank][:, c * 128:(c + 1) * 128],
                                        xsg4[k][:, cc * 128:(cc + 1) * 128],
                                        ident[:, :], start=True, stop=True,
                                        skip_group_check=True)
                                return i
                            emit("tensor", ktr,
                                 waits=[(sDV.h, mk_tt[gidx]),
                                        (sDV.h, mk_kev.get((sb - 1, g4), 0))],
                                 inc=sPE)
                            mk_ktr[(gidx, f)] = sPE.n
                            mk_kev[(sb, g4)] = emit(
                                "vector", lambda e, sb=sb, g4=g4, bank=bank:
                                e.tensor_copy(
                                    out=K[:, sb * 1024 + g4 * 512:
                                          sb * 1024 + (g4 + 1) * 512],
                                    in_=pb[bank][:, 0:512]),
                                waits=[(sPE.h, mk_ktr[(gidx, f)])], inc=sDV)
                        mk_kev[sb] = mk_kev[(sb, 1)]

                pb_guard = {0: (sDV.h, mk_kev[(31, 0)]),
                            1: (sDV.h, mk_kev[(31, 1)])}
                pbav_free = {i: None for i in range(2, 8)}
                attn_user = {0: 0, 1: 0}
                sst_store = {0: None, 1: None}
                mk_qld = {}
                mk_scores_done = {}
                for pi, pack in enumerate(PACKS[:NPK]):
                    sz = len(pack)
                    Np = sz * 128
                    nsb = n_sb_of(pack)
                    relbase = 2 * pack[0]
                    kq = pi % 2
                    for j, o_ in enumerate(pack):
                        mk_qld[pi] = dma(
                            "sync",
                            Qp[kq][:, 0:8 * sz * 128].rearrange(
                                "p (c o t) -> p c o t", c=8, o=sz)[
                                :, :, j:j + 1, :].squeeze(2),
                            qfm_s[8 * h:8 * h + 8, o_, :, :].rearrange(
                                "c d t -> d c t"),
                            waits=[(sPE.h, mk_scores_done.get(pi - 2, 0))])

                    pend_av = []
                    mk_attn_ready = {}
                    mk_av = {}
                    for sb in range(0 if NOSC else nsb):
                        bank = sb % 2
                        def smm(e, sb=sb, bank=bank, kq=kq, Np=Np, sz=sz):
                            i = None
                            for c in range(8):
                                i = e.matmul(
                                    pb[bank][:, 0:Np],
                                    K[:, sb * 1024 + c * 128:
                                      sb * 1024 + (c + 1) * 128],
                                    Qp[kq][:, c * sz * 128:(c + 1) * sz * 128],
                                    start=(c == 0), stop=(c == 7),
                                    skip_group_check=True)
                            return i
                        emit("tensor", smm,
                             waits=[mk_qld[pi], (sDV.h, mk_kev[sb]),
                                    pb_guard[bank]], inc=sPE)
                        mk_sc = sPE.n
                        if sb == nsb - 1:
                            mk_scores_done[pi] = mk_sc
                        ka = sb % 2
                        if NOSIG:
                            continue
                        mk_sig = emit(
                            "scalar", lambda e, bank=bank, ka=ka, Np=Np, h=h:
                            e.activation(attnT[ka][:, 0:Np], pb[bank][:, 0:Np],
                                         AF.Sigmoid, scale=sigsc[:, h:h + 1]),
                            waits=[(sPE.h, mk_sc), (sPE.h, attn_user[ka])],
                            inc=sAC)
                        pb_guard[bank] = (sAC.h, mk_sig)
                        rel = sb - relbase
                        if rel >= 0 and not NOMASK:
                            mview = (m1[:, rel * 128:(rel + 1) * 128] if sz == 1
                                     else m3[:, rel * 384:(rel + 1) * 384])
                            mk_msk = emit(
                                "vector", lambda e, ka=ka, Np=Np, mview=mview:
                                e.tensor_tensor(out=attnT[ka][:, 0:Np],
                                                in0=attnT[ka][:, 0:Np],
                                                in1=mview, op=ALU.mult),
                                waits=[(sAC.h, mk_sig)], inc=sDV)
                            mk_attn_ready[sb] = (sDV.h, mk_msk)
                        else:
                            mk_attn_ready[sb] = (sAC.h, mk_sig)

                        def avmm(e, sb=sb, ka=ka, pack=pack):
                            i = None
                            for j, o in enumerate(pack):
                                if sb < n_av_of(o):
                                    for half in range(2):
                                        i = e.matmul(
                                            pb[2 + 2 * j + half][:, :],
                                            attnT[ka][:, j * 128:(j + 1) * 128],
                                            V[:, sb * 1024 + half * 512:
                                              sb * 1024 + (half + 1) * 512],
                                            start=(sb == 0),
                                            stop=(sb == n_av_of(o) - 1),
                                            skip_group_check=True)
                            return i
                        if NOAV:
                            continue
                        avw = [mk_attn_ready[sb]]
                        if sb == 0:
                            avw += [pbav_free[i] for i in range(2, 2 + 2 * sz)]
                        pend_av.append((avmm, avw, sb))
                        if len(pend_av) >= 2:
                            fn_, w_, sb_ = pend_av.pop(0)
                            emit("tensor", fn_, waits=w_, inc=sPE)
                            mk_av[sb_] = sPE.n
                            attn_user[sb_ % 2] = sPE.n
                    while pend_av:
                        fn_, w_, sb_ = pend_av.pop(0)
                        emit("tensor", fn_, waits=w_, inc=sPE)
                        mk_av[sb_] = sPE.n
                        attn_user[sb_ % 2] = sPE.n

                    for j, o in enumerate(pack):
                        if NOAV or NOSIG or NOSC:
                            break
                        ks = j % 2
                        last = mk_av[n_av_of(o) - 1]
                        mk_eh = emit("scalar", lambda e, j=j, ks=ks, h=h:
                                     e.activation(sst[ks][:, 0:512],
                                                  pb[2 + 2 * j][:, :], AF.Identity,
                                                  scale=avsc[:, h:h + 1]),
                                     waits=[(sPE.h, last),
                                            sst_store[ks]], inc=sAC)
                        pbav_free[2 + 2 * j] = (sAC.h, mk_eh)
                        mk_ev = emit("vector", lambda e, j=j, ks=ks, h=h:
                                     e.tensor_scalar(
                                         out=sst[ks][:, 512:1024],
                                         in0=pb[2 + 2 * j + 1][:, :],
                                         scalar1=avsc[:, h:h + 1], scalar2=None,
                                         op0=ALU.mult),
                                     waits=[(sPE.h, last),
                                            sst_store[ks]], inc=sDV)
                        pbav_free[2 + 2 * j + 1] = (sDV.h, mk_ev)
                        sst_store[ks] = dma(
                            "gpsimd", s_s[o, :, h, :], sst[ks][:, :],
                            waits=[(sAC.h, mk_eh), (sDV.h, mk_ev)])
        barrier()

        with contextlib.ExitStack() as p3:
            PE3 = p3.enter_context
            xq3 = [PE3(nc.sbuf_tensor(f"xq3_{i}", [128, D], F32)) for i in range(2)]
            s3 = [PE3(nc.sbuf_tensor(f"s3_{i}", [128, D], F32)) for i in range(2)]
            xn3 = [PE3(nc.sbuf_tensor(f"xn3_{i}", [128, D], BF16)) for i in range(2)]
            y3 = [PE3(nc.sbuf_tensor(f"y3_{i}", [128, 1024], F32)) for i in range(2)]
            yb3 = [PE3(nc.sbuf_tensor(f"yb3_{i}", [128, 1024], BF16)) for i in range(2)]
            dg = [PE3(nc.sbuf_tensor(f"dg{i}", [128, 20 * 128], BF16)) for i in range(2)]
            gb = [PE3(nc.sbuf_tensor(f"gb{i}", [128, D], BF16)) for i in range(2)]
            outst = [PE3(nc.sbuf_tensor(f"outst{i}", [128, D], F32)) for i in range(2)]

            emit("scalar", lambda e: e.activation(
                warm[:, 0:1], epst[:, 0:1], AF.Silu), inc=sAC)
            mk_out = {}
            mk_ffn = {}
            mk_fin_dv = {}
            mk_fin_po = {}
            mk_sg = {}
            mk_silu = {}
            for o in range(0 if 'p3' in SKIP else OB):
                k = o % 2
                m_xq = dma("sync", xq3[k][:, :],
                           xq[o * 128:(o + 1) * 128, :],
                           waits=[(sDV.h, mk_fin_dv.get(o - 2, 0)),
                                  (sPO.h, mk_fin_po.get(o - 2, 0))])
                m_s = dma("sync", s3[k][:, :],
                          s_s[o, :, :, :].rearrange("p h d -> p (h d)"),
                          waits=[(sDV.h, mk_fin_dv.get(o - 2, 0)),
                                 (sPO.h, mk_fin_po.get(o - 2, 0))])
                m_xn = dma("gpsimd", xn3[k][:, :],
                           xnq_s[o * 128:(o + 1) * 128, :],
                           waits=[(sPE.h, mk_ffn.get(o - 2, 0))])
                m_y = dma("gpsimd", y3[k][:, :],
                          y_s[o * 128:(o + 1) * 128, :],
                          waits=[(sDV.h, mk_fin_dv.get(o - 2, 0))])
                def dgb(e, o=o, k=k):
                    for t in range(16):
                        e.tensor_scalar(out=dg[k][:, t * 128:(t + 1) * 128],
                                        in0=ident[:, :],
                                        scalar1=Hres[:, o * 16 + t:o * 16 + t + 1],
                                        scalar2=None, op0=ALU.mult)
                    i = None
                    for t in range(4):
                        i = e.tensor_scalar(
                            out=dg[k][:, (16 + t) * 128:(17 + t) * 128],
                            in0=ident[:, :],
                            scalar1=Hpost[:, o * 4 + t:o * 4 + t + 1],
                            scalar2=None, op0=ALU.mult)
                    return i
                emit("vector", dgb, waits=[(sPE.h, mk_ffn.get(o - 2, 0))], inc=sDV)
                emit("vector", lambda e, k=k: e.tensor_copy(
                    out=yb3[k][:, :], in_=y3[k][:, :]),
                    waits=[m_y, (sPE.h, mk_ffn.get(o - 2, 0))], inc=sDV)
                mk_dg = sDV.n
                for i4 in range(4):
                    for half in range(2):
                        bank = i4 * 2 + half
                        def fmm(e, i4=i4, half=half, bank=bank, k=k):
                            for j in range(4):
                                e.matmul(pb[bank][:, :],
                                         dg[k][:, (i4 * 4 + j) * 128:
                                               (i4 * 4 + j + 1) * 128],
                                         xn3[k][:, j * 1024 + half * 512:
                                                j * 1024 + (half + 1) * 512],
                                         start=(j == 0), stop=False,
                                         skip_group_check=True)
                            return e.matmul(pb[bank][:, :],
                                            dg[k][:, (16 + i4) * 128:
                                                  (17 + i4) * 128],
                                            yb3[k][:, half * 512:(half + 1) * 512],
                                            start=False, stop=True,
                                            skip_group_check=True)
                        emit("tensor", fmm,
                             waits=[(sDV.h, mk_dg), m_xn,
                                    (sAC.h, mk_silu.get((o - 1, i4, half), 0))],
                             inc=sPE)
                        mk_f = sPE.n
                        mk_silu[(o, i4, half)] = emit(
                            "scalar", lambda e, i4=i4, half=half, bank=bank, k=k:
                            e.activation(
                                gb[k][:, i4 * 1024 + half * 512:
                                      i4 * 1024 + (half + 1) * 512],
                                pb[bank][:, :], AF.Silu),
                            waits=[(sPE.h, mk_f), (sDV.h, mk_sg.get(o - 2, 0))],
                            inc=sAC)
                mk_ffn[o] = sPE.n
                mk_g = sAC.n
                if KDBG:
                    dma("gpsimd", dbg_g[o * 128:(o + 1) * 128, :],
                        gb[k][:, :], waits=[(sAC.h, mk_g)])
                emit("vector", lambda e, k=k: e.tensor_tensor(
                    out=s3[k][:, :], in0=s3[k][:, :], in1=gb[k][:, :],
                    op=ALU.mult),
                    waits=[(sAC.h, mk_g), m_s], inc=sDV)
                mk_sg[o] = sDV.n
                if NOPOOL:
                    emit("vector", lambda e, k=k: e.tensor_tensor(
                        out=outst[k][:, :], in0=xq3[k][:, :],
                        in1=s3[k][:, :], op=ALU.add),
                        waits=[m_xq, mk_out.get(o - 2)],
                        inc=sDV)
                    mk_fin_dv[o] = sDV.n
                    mk_fin_po[o] = 0
                else:
                    emit("vector", lambda e, k=k: e.tensor_tensor(
                        out=outst[k][:, 0:2048], in0=xq3[k][:, 0:2048],
                        in1=s3[k][:, 0:2048], op=ALU.add),
                        waits=[m_xq, mk_out.get(o - 2)], inc=sDV)
                    mk_fin_dv[o] = sDV.n
                    emit("gpsimd", lambda e, k=k: e.tensor_tensor(
                        out=outst[k][:, 2048:4096], in0=xq3[k][:, 2048:4096],
                        in1=s3[k][:, 2048:4096], op=ALU.add),
                        waits=[(sDV.h, mk_sg[o]), m_xq,
                               mk_out.get(o - 2)], inc=sPO)
                    mk_fin_po[o] = sPO.n
                mk_out[o] = dma("sync", out[o * 128:(o + 1) * 128, :],
                                outst[k][:, :],
                                waits=[(sDV.h, mk_fin_dv[o]),
                                       (sPO.h, mk_fin_po[o])])

            if KDBG:
                barrier()
                dma("sync", dbg_xn[:, :], xn_s[:, :])
                dma("sync", dbg_xnq[:, :], xnq_s[:, :])
                dma("sync", dbg_z[:, :], Z[:, :])
                dma("sync", dbg_h[:, 0:OB * 16], Hres[:, :])
                dma("sync", dbg_h[:, OB * 16:OB * 20], Hpre[:, :])
                dma("sync", dbg_h[:, OB * 20:OB * 24], Hpost[:, :])
                dma("sync", dbg_y[:, :], y_s[:, :])
                dma("sync", dbg_ss[:, :, :, :], s_s[:, :, :, :])
                dma("sync", dbg_q[:, :, :, :], qfm_s[:, :, :, :])
            emit("sync", lambda e: None,
                 waits=[(s.h, s.n) for q in (qS, qG) for s in q.slots])

        block = GE(nc.Block())
        for eng_name in ENGS:
            def body(e, eng_name=eng_name):
                done = {}
                for waits, fn, tgt in prog[eng_name]:
                    for (hh, v) in waits:
                        key = id(hh)
                        if done.get(key, 0) < v:
                            e.wait_ge(hh, v)
                            done[key] = v
                    inst = fn(e)
                    if tgt is not None:
                        inst.then_inc(tgt[0], tgt[1])
            getattr(block, eng_name)(body)
    return nc


def _host_masks(r):
    s = np.arange(128)[:, None]
    t = np.arange(128)[None, :]
    m1 = np.zeros((2, 128, 128), np.float32)
    for rel in range(2):
        m1[rel] = ((r - rel) * 128 + t - s >= 0).astype(np.float32)
    m3 = np.zeros((6, 128, 384), np.float32)
    for rel in range(6):
        for j in range(3):
            m3[rel][:, j * 128:(j + 1) * 128] = \
                ((2 * j + r - rel) * 128 + t - s >= 0).astype(np.float32)
    return m1, m3


_NC_CACHE = {}


def kernel(L0_attn_x, L0_attn_bv_q, L0_attn_bv_k, L0_attn_bv_v,
           L0_ffn_W_res, L0_ffn_W_pre, L0_ffn_W_post, L0_ffn_H_res_base,
           L0_ffn_H_pre_base, L0_ffn_H_post_base, L0_ffn_gate, L0_ffn_W_branch):
    x = np.ascontiguousarray(np.asarray(L0_attn_x, dtype=np.float32))
    wcat = np.ascontiguousarray(np.concatenate([
        np.asarray(L0_ffn_W_res, np.float32),
        np.asarray(L0_ffn_W_pre, np.float32),
        np.asarray(L0_ffn_W_post, np.float32)], axis=0))
    basecat = np.ascontiguousarray(np.concatenate([
        np.asarray(L0_ffn_H_res_base, np.float32).reshape(16),
        np.asarray(L0_ffn_H_pre_base, np.float32).reshape(4),
        np.asarray(L0_ffn_H_post_base, np.float32).reshape(4)]).reshape(1, 24))
    gate = np.asarray(L0_ffn_gate, np.float32).reshape(1, 1)
    bvq = np.ascontiguousarray(np.asarray(L0_attn_bv_q, np.float32))
    bvk = np.ascontiguousarray(np.asarray(L0_attn_bv_k, np.float32))
    bvv = np.ascontiguousarray(np.asarray(L0_attn_bv_v, np.float32))
    wbr = np.ascontiguousarray(np.asarray(L0_ffn_W_branch, np.float32))

    if "nc" not in _NC_CACHE:
        _NC_CACHE["nc"] = build_program()
    nc = _NC_CACHE["nc"]

    in_maps = []
    for c in range(8):
        b, r = c // 2, c % 2
        xb = x[b]
        rows = np.concatenate([np.arange((2 * o + r) * 128,
                                         (2 * o + r) * 128 + 128)
                               for o in range(OB)])
        m1h, m3h = _host_masks(r)
        in_maps.append({
            "xfull": xb,
            "xq": np.ascontiguousarray(xb[rows]),
            "mask1": m1h, "mask3": m3h,
            "bvq": bvq, "bvk": bvk, "bvv": bvv,
            "wcat": wcat, "basecat": basecat, "gate": gate, "wbr": wbr,
        })
    res = run_bass_kernel_spmd(nc, in_maps, core_ids=list(range(8)))
    outf = np.empty((4, T, D), dtype=np.float32)
    for c in range(8):
        b, r = c // 2, c % 2
        o_c = np.asarray(res.results[c]["out"], dtype=np.float32)
        for o in range(OB):
            outf[b, (2 * o + r) * 128:(2 * o + r) * 128 + 128, :] = \
                o_c[o * 128:(o + 1) * 128, :]
    return outf


# version 05
# speedup vs baseline: 1.0653x; 1.0653x over previous
import contextlib
import os
import numpy as np
import ml_dtypes
import concourse.bass as bass
import concourse.mybir as mybir
from concourse.bass_utils import run_bass_kernel_spmd

F32 = mybir.dt.float32
BF16 = mybir.dt.bfloat16
AF = mybir.ActivationFunctionType
ALU = mybir.AluOpType
AX = mybir.AxisListType

T = 4096
D = 4096
H = 4
HD = 1024
NB = 32
OB = 16
OT = OB * 128
EPS1 = 1e-5
EPS2 = 1.1920929e-07
ENGS = ("sync", "tensor", "vector", "scalar", "gpsimd")

PACKS = [[0]] + [[3 * q - 2, 3 * q - 1, 3 * q] for q in range(1, 6)]


def n_sb_of(pack):
    return 2 * pack[-1] + 2


def n_av_of(o):
    return 2 * o + 2


def build_program():
    SKIP = set(os.environ.get("KSKIP", "").split(","))
    NH = int(os.environ.get("KP2H", "4"))
    NPK = int(os.environ.get("KP2P", "6"))
    NOAV = "noav" in SKIP
    NOSIG = "nosig" in SKIP
    NOMASK = "nomask" in SKIP
    NOSC = "nosc" in SKIP
    NOPOOL = True
    KDBG = "1" == os.environ.get("KDBG", "")
    nc = bass.Bass("TRN2", target_bir_lowering=False)

    xfull = nc.dram_tensor("xfull", [T, D], F32, kind="ExternalInput").ap()
    xq = nc.dram_tensor("xq", [OT, D], F32, kind="ExternalInput").ap()
    mask1 = nc.dram_tensor("mask1", [2, 128, 128], F32, kind="ExternalInput").ap()
    mask3 = nc.dram_tensor("mask3", [6, 128, 384], F32, kind="ExternalInput").ap()
    bvq = nc.dram_tensor("bvq", [4, HD], F32, kind="ExternalInput").ap()
    bvk = nc.dram_tensor("bvk", [4, HD], F32, kind="ExternalInput").ap()
    bvv = nc.dram_tensor("bvv", [4, HD], F32, kind="ExternalInput").ap()
    wcat = nc.dram_tensor("wcat", [24, D], F32, kind="ExternalInput").ap()
    basecat = nc.dram_tensor("basecat", [1, 24], F32, kind="ExternalInput").ap()
    gate = nc.dram_tensor("gate", [1, 1], F32, kind="ExternalInput").ap()
    wbr = nc.dram_tensor("wbr", [1024, 1024], F32, kind="ExternalInput").ap()
    out = nc.dram_tensor("out", [OT, D], F32, kind="ExternalOutput").ap()

    xn_s = nc.dram_tensor("xn_s", [T, D], BF16, kind="Internal").ap()
    xnq_s = nc.dram_tensor("xnq_s", [OT, D], BF16, kind="Internal").ap()
    qfm_s = nc.dram_tensor("qfm_s", [32, OB, 128, 128], BF16, kind="Internal").ap()
    y_s = nc.dram_tensor("y_s", [OT, 1024], F32, kind="Internal").ap()
    s_s = nc.dram_tensor("s_s", [OB, 128, H, 1024], BF16, kind="Internal").ap()
    sgnqk_s = nc.dram_tensor("sgnqk_s", [4, HD], BF16, kind="Internal").ap()
    sgnv_s = nc.dram_tensor("sgnv_s", [4, HD], BF16, kind="Internal").ap()
    asc_s = nc.dram_tensor("asc_s", [4, 1], F32, kind="Internal").ap()
    avsc_s = nc.dram_tensor("avsc_s", [4, 1], F32, kind="Internal").ap()
    g_s = nc.dram_tensor("g_s", [1, 2], F32, kind="Internal").ap()
    KDBG_ = "1" == os.environ.get("KDBG", "")
    if KDBG_:
        dbg_xn = nc.dram_tensor("dbg_xn", [T, D], BF16, kind="ExternalOutput").ap()
        dbg_xnq = nc.dram_tensor("dbg_xnq", [OT, D], BF16, kind="ExternalOutput").ap()
        dbg_z = nc.dram_tensor("dbg_z", [128, OB * 24], F32, kind="ExternalOutput").ap()
        dbg_h = nc.dram_tensor("dbg_h", [128, OB * 24], F32, kind="ExternalOutput").ap()
        dbg_y = nc.dram_tensor("dbg_y", [OT, 1024], F32, kind="ExternalOutput").ap()
        dbg_ss = nc.dram_tensor("dbg_ss", [OB, 128, H, 1024], BF16, kind="ExternalOutput").ap()
        dbg_q = nc.dram_tensor("dbg_q", [32, OB, 128, 128], BF16, kind="ExternalOutput").ap()
        dbg_g = nc.dram_tensor("dbg_g", [OT, D], BF16, kind="ExternalOutput").ap()

    class SemC:
        def __init__(self, h):
            self.h = h
            self.n = 0

    prog = {e: [] for e in ENGS}
    pending = {e: [] for e in ENGS}

    def emit(eng, fn, waits=(), inc=None, k=1):
        w = pending[eng] + [x for x in waits if x is not None and x[1] > 0]
        pending[eng] = []
        tgt = None
        if inc is not None:
            inc.n += k
            tgt = (inc.h, k)
        prog[eng].append((w, fn, tgt))
        return inc.n if inc is not None else None

    with contextlib.ExitStack() as glob:
        GE = glob.enter_context
        sPE = SemC(GE(nc.semaphore("sPE")))
        sDV = SemC(GE(nc.semaphore("sDV")))
        sAC = SemC(GE(nc.semaphore("sAC")))
        sPO = SemC(GE(nc.semaphore("sPO")))
        ALL = (sPE, sDV, sAC, sPO)

        class DmaQ:
            def __init__(self, name, eng, R=6):
                self.eng = eng
                self.R = R
                self.slots = [SemC(GE(nc.semaphore(f"{name}{i}")))
                              for i in range(R)]
                self.count = 0

        qS = DmaQ("qS", "sync")
        qG = DmaQ("qG", "gpsimd")

        def barrier():
            snap = [(s.h, s.n) for s in ALL if s.n > 0]
            snap += [(s.h, s.n) for q in (qS, qG) for s in q.slots if s.n > 0]
            for e in ENGS:
                pending[e] = pending[e] + snap

        def dma(eng, o_, i_, waits=()):
            q = qS if eng == "sync" else qG
            slot = q.slots[q.count % q.R]
            q.count += 1
            prevfull = slot.n

            def fn(e, o_=o_, i_=i_):
                return e.dma_start(out=o_, in_=i_)
            emit(q.eng, fn, list(waits) + [(slot.h, prevfull)], inc=slot, k=16)
            return (slot.h, slot.n)

        pb = [GE(nc.psum_tensor(f"pb{i}", [128, 512], F32)) for i in range(8)]
        pbb = [p.ap().bitcast(BF16) for p in pb]

        ident = GE(nc.sbuf_tensor("ident", [128, 128], BF16))
        m1 = GE(nc.sbuf_tensor("m1", [128, 2 * 128], BF16))
        m3 = GE(nc.sbuf_tensor("m3", [128, 6 * 384], BF16))
        wcat_fm = GE(nc.sbuf_tensor("wcat_fm", [128, 32 * 24], BF16))
        base10 = GE(nc.sbuf_tensor("base10", [128, 16], F32))
        baseA = GE(nc.sbuf_tensor("baseA", [128, 24], F32))
        sigsc = GE(nc.sbuf_tensor("sigsc", [128, 4], F32))
        avsc = GE(nc.sbuf_tensor("avsc", [128, 4], F32))
        gomg = GE(nc.sbuf_tensor("gomg", [128, 2], F32))
        Z = GE(nc.sbuf_tensor("Z", [128, OB * 24], F32))
        Hres = GE(nc.sbuf_tensor("Hres", [128, OB * 16], F32))
        Hpre = GE(nc.sbuf_tensor("Hpre", [128, OB * 4], F32))
        Hpost = GE(nc.sbuf_tensor("Hpost", [128, OB * 4], F32))
        nu = GE(nc.sbuf_tensor("nu", [128, 64], F32))
        nv = GE(nc.sbuf_tensor("nv", [128, 64], F32))
        tsk = GE(nc.sbuf_tensor("tsk", [128, 256], F32))
        epst = GE(nc.sbuf_tensor("epst", [128, 2], F32))
        warm = GE(nc.sbuf_tensor("warm", [128, 2], F32))
        esk = GE(nc.sbuf_tensor("esk", [128, 256], F32))
        ssk = GE(nc.sbuf_tensor("ssk", [128, 64], F32))

        with contextlib.ExitStack() as p0:
            PE0 = p0.enter_context
            bq = PE0(nc.sbuf_tensor("bq", [4, HD], F32))
            bk = PE0(nc.sbuf_tensor("bk", [4, HD], F32))
            bv = PE0(nc.sbuf_tensor("bv", [4, HD], F32))
            tq = PE0(nc.sbuf_tensor("tq", [4, HD], F32))
            tk = PE0(nc.sbuf_tensor("tk", [4, HD], F32))
            tv_ = PE0(nc.sbuf_tensor("tv_", [4, HD], F32))
            al4 = PE0(nc.sbuf_tensor("al4", [4, 4], F32))
            gt = PE0(nc.sbuf_tensor("gt", [1, 2], F32))
            wc32 = PE0(nc.sbuf_tensor("wc32", [24, D], F32))
            wcb = PE0(nc.sbuf_tensor("wcb", [24, D], BF16))

            m_bq = dma("sync", bq[:, :], bvq[:, :])
            m_bk = dma("sync", bk[:, :], bvk[:, :])
            m_bv = dma("sync", bv[:, :], bvv[:, :])
            m_wc = dma("sync", wc32[:, :], wcat[:, :])
            m_gt = dma("sync", gt[:, 0:1], gate[:, :])
            dma("gpsimd", m1[:, :].rearrange("p (k t) -> p k t", k=2),
                mask1[:, :, :].rearrange("k s t -> s k t"))
            dma("gpsimd", m3[:, :].rearrange("p (k t) -> p k t", k=6),
                mask3[:, :, :].rearrange("k s t -> s k t"))
            m_ba = dma("sync", baseA[:, :], basecat[0:1, :].broadcast_to((128, 24)))

            emit("gpsimd", lambda e: e.memset(ident[:, :], 0.0))
            emit("vector", lambda e: e.memset(epst[:, 0:1], EPS1), inc=sDV)
            emit("vector", lambda e: e.memset(epst[:, 1:2], EPS2), inc=sDV)
            emit("gpsimd", lambda e: e.affine_select(
                out=ident[:, :], in_=ident[:, :], compare_op=ALU.not_equal,
                fill=1.0, base=0, pattern=[[-1, 128]], channel_multiplier=1),
                inc=sPO)
            mk_id = sPO.n

            emit("vector", lambda e: e.tensor_scalar(
                out=base10[:, :], in0=baseA[:, 0:16], scalar1=10.0, scalar2=None,
                op0=ALU.mult), waits=[m_ba], inc=sDV)

            for src, col, wm in ((bq, 0, m_bq), (bk, 1, m_bk), (bv, 2, m_bv)):
                emit("vector", lambda e, src=src, col=col: e.reduce_sum(
                    out=al4[:, col:col + 1], in_=src[:, :], axis=AX.X,
                    apply_absolute_value=True), waits=[wm], inc=sDV)
            mk_alraw = sDV.n
            emit("vector", lambda e: e.tensor_mul(
                al4[:, 3:4], al4[:, 0:1], al4[:, 1:2]),
                waits=[(sDV.h, mk_alraw)], inc=sDV)
            mk_t1 = sDV.n
            emit("vector", lambda e: e.tensor_scalar(
                out=al4[:, 3:4], in0=al4[:, 3:4],
                scalar1=0.125 / (1024.0 * 1024.0), scalar2=None, op0=ALU.mult),
                waits=[(sDV.h, mk_t1)], inc=sDV)
            emit("vector", lambda e: e.tensor_scalar(
                out=al4[:, 2:3], in0=al4[:, 2:3], scalar1=1.0 / 1024.0,
                scalar2=None, op0=ALU.mult), waits=[(sDV.h, mk_alraw)], inc=sDV)
            mk_al = sDV.n
            emit("scalar", lambda e: e.activation(
                gt[:, 0:1], gt[:, 0:1], AF.Sigmoid), waits=[m_gt], inc=sAC)
            emit("vector", lambda e: e.tensor_scalar(
                out=gt[:, 1:2], in0=gt[:, 0:1], scalar1=-1.0, scalar2=1.0,
                op0=ALU.mult, op1=ALU.add), waits=[(sAC.h, sAC.n)], inc=sDV)
            mk_gt = sDV.n

            pending["scalar"] = pending["scalar"] + [(sDV.h, mk_alraw)]
            for src, tmp in ((bq, tq), (bk, tk), (bv, tv_)):
                emit("scalar", lambda e, src=src: e.activation(
                    src[:, :], src[:, :], AF.Sign), inc=sAC)
                emit("scalar", lambda e, src=src, tmp=tmp: e.activation(
                    tmp[:, :], src[:, :], AF.Square), inc=sAC)
                mk_sq2 = sAC.n
                emit("vector", lambda e, src=src, tmp=tmp: e.tensor_sub(
                    src[:, :], src[:, :], tmp[:, :]), waits=[(sAC.h, mk_sq2)],
                    inc=sDV)
                emit("vector", lambda e, src=src: e.tensor_scalar(
                    out=src[:, :], in0=src[:, :], scalar1=1.0, scalar2=None,
                    op0=ALU.add), inc=sDV)
            emit("vector", lambda e: e.tensor_mul(bq[:, :], bq[:, :], bk[:, :]),
                 inc=sDV)
            mk_sgn = sDV.n

            m_sq = dma("gpsimd", sgnqk_s[:, :], bq[:, :], waits=[(sDV.h, mk_sgn)])
            m_sv = dma("gpsimd", sgnv_s[:, :], bv[:, :], waits=[(sDV.h, mk_sgn)])
            m_as = dma("gpsimd", asc_s[:, :], al4[:, 3:4], waits=[(sDV.h, mk_al)])
            m_av = dma("gpsimd", avsc_s[:, :], al4[:, 2:3], waits=[(sDV.h, mk_al)])
            m_gs = dma("gpsimd", g_s[0:1, :], gt[0:1, :], waits=[(sDV.h, mk_gt)])
            dma("sync", sigsc[:, :],
                asc_s[:, :].rearrange("a b -> b a").broadcast_to((128, 4)),
                waits=[m_as])
            dma("sync", avsc[:, :],
                avsc_s[:, :].rearrange("a b -> b a").broadcast_to((128, 4)),
                waits=[m_av])
            dma("sync", gomg[:, :], g_s[0:1, :].broadcast_to((128, 2)),
                waits=[m_gs])

            emit("vector", lambda e: e.tensor_copy(out=wcb[:, :], in_=wc32[:, :]),
                 waits=[m_wc], inc=sDV)
            mk_wcb = sDV.n
            for half in range(2):
                def tr16(e, half=half):
                    i = None
                    for c in range(16):
                        cc = half * 16 + c
                        i = e.transpose(pbb[half][:, c * 24:(c + 1) * 24],
                                        wcb[:, cc * 128:(cc + 1) * 128],
                                        ident[0:24, 0:24])
                    return i
                emit("tensor", tr16, waits=[(sDV.h, mk_wcb), (sPO.h, mk_id)],
                     inc=sPE)
                mk_t = sPE.n
                emit("scalar", lambda e, half=half: e.activation(
                    wcat_fm[:, half * 384:(half + 1) * 384],
                    pbb[half][:, 0:384], AF.Identity),
                    waits=[(sPE.h, mk_t)], inc=sAC)
        barrier()

        with contextlib.ExitStack() as p1:
            emit("scalar", lambda e: e.activation(
                warm[:, 0:1], epst[:, 0:1], AF.Ln), inc=sAC)
            emit("scalar", lambda e: e.activation(
                warm[:, 0:1], epst[:, 0:1], AF.Exp), inc=sAC)
            PE1 = p1.enter_context
            xst = [PE1(nc.sbuf_tensor(f"xst{i}", [128, D], F32)) for i in range(2)]
            xnt = [PE1(nc.sbuf_tensor(f"xnt{i}", [128, D], BF16)) for i in range(2)]
            st6 = [PE1(nc.sbuf_tensor(f"st6_{i}", [128, 48], F32)) for i in range(2)]
            stv = [PE1(nc.sbuf_tensor(f"stv{i}", [128, 8], F32)) for i in range(2)]

            mark_apply = {}
            mark_store = {}
            for i in range(0 if 'p1' in SKIP else NB):
                k = i % 2
                mld = dma("sync", xst[k][:, :],
                          xfull[i * 128:(i + 1) * 128, :],
                          waits=[(sAC.h, mark_apply.get(i - 2, 0))])
                def bns(e, k=k):
                    i_ = None
                    for j in range(8):
                        i_ = e.bn_stats(out=st6[k][:, j * 6:(j + 1) * 6],
                                        in_=xst[k][:, j * 512:(j + 1) * 512])
                    return i_
                emit("vector", bns, waits=[mld], inc=sDV)
                mk_bns = sDV.n
                emit("vector", lambda e, k=k: e.bn_aggr(
                    out=stv[k][:, 0:2], in_=st6[k][:, :]),
                    waits=[(sDV.h, mk_bns)], inc=sDV)
                mk_bn = sDV.n
                emit("scalar", lambda e, k=k: e.activation(
                    stv[k][:, 2:3], stv[k][:, 1:2], AF.Ln, bias=epst[:, 0:1]),
                    waits=[(sDV.h, mk_bn)], inc=sAC)
                mk_ln = sAC.n
                emit("scalar", lambda e, k=k: e.activation(
                    stv[k][:, 2:3], stv[k][:, 2:3], AF.Exp, scale=-0.5),
                    waits=[(sAC.h, mk_ln)], inc=sAC)
                mk_rstd = sAC.n
                emit("vector", lambda e, k=k: e.tensor_scalar(
                    out=stv[k][:, 3:4], in0=stv[k][:, 0:1], scalar1=stv[k][:, 2:3],
                    scalar2=-1.0, op0=ALU.mult, op1=ALU.mult),
                    waits=[(sAC.h, mk_rstd), (sDV.h, mk_bn)], inc=sDV)
                mk_nm = sDV.n
                mark_apply[i] = emit("scalar", lambda e, k=k: e.activation(
                    xnt[k][:, :], xst[k][:, :], AF.Identity,
                    bias=stv[k][:, 3:4], scale=stv[k][:, 2:3]),
                    waits=[(sDV.h, mk_nm), (sAC.h, mk_rstd),
                           mark_store.get(i - 2)],
                    inc=sAC)
                mark_store[i] = dma("gpsimd",
                                    xn_s[i * 128:(i + 1) * 128, :],
                                    xnt[k][:, :],
                                    waits=[(sAC.h, mark_apply[i])])
        barrier()

        with contextlib.ExitStack() as p15:
            PE15 = p15.enter_context
            xqst = [PE15(nc.sbuf_tensor(f"xqst{i}", [128, D], F32)) for i in range(2)]
            xnt2 = [PE15(nc.sbuf_tensor(f"xnt2_{i}", [128, D], BF16)) for i in range(2)]
            xnfm = [PE15(nc.sbuf_tensor(f"xnfm{i}", [128, 32 * 128], BF16)) for i in range(2)]
            st6b = PE15(nc.sbuf_tensor("st6b", [128, 48], F32))
            stvb = [PE15(nc.sbuf_tensor(f"stvb{i}", [128, 8], F32)) for i in range(2)]
            wb_fm = PE15(nc.sbuf_tensor("wb_fm", [128, 8 * 1024], BF16))
            wb_st = PE15(nc.sbuf_tensor("wb_st", [128, 1024], F32))
            acc = [PE15(nc.sbuf_tensor(f"acc{i}", [128, 1024], F32)) for i in range(2)]
            xpb = [PE15(nc.sbuf_tensor(f"xpb{i}", [128, 1024], BF16)) for i in range(2)]
            xpfm = [PE15(nc.sbuf_tensor(f"xpfm{i}", [128, 1024], BF16)) for i in range(2)]
            yst = [PE15(nc.sbuf_tensor(f"yst{i}", [128, 1024], F32)) for i in range(2)]

            mk_wb = {}
            for c in range(8):
                ml = dma("sync", wb_st[:, :], wbr[c * 128:(c + 1) * 128, :],
                         waits=[(sDV.h, mk_wb.get(c - 1, 0))])
                mk_wb[c] = emit("vector", lambda e, c=c: e.tensor_copy(
                    out=wb_fm[:, c * 1024:(c + 1) * 1024], in_=wb_st[:, :]),
                    waits=[ml], inc=sDV)

            mk_apl = {}
            mk_xnqst = {}
            _OB_A = 0 if 'p15a' in SKIP else OB
            mk_qdma = {}
            mk_evac = {}
            mk_trlast = {}
            mk_logit_evac = {}
            for o in range(_OB_A):
                k = o % 2
                mld = dma("sync", xqst[k][:, :],
                          xq[o * 128:(o + 1) * 128, :],
                          waits=[(sAC.h, mk_apl.get(o - 2, 0))])
                def bns2(e, k=k):
                    i_ = None
                    for j in range(8):
                        i_ = e.bn_stats(out=st6b[:, j * 6:(j + 1) * 6],
                                        in_=xqst[k][:, j * 512:(j + 1) * 512])
                    return i_
                emit("vector", bns2, waits=[mld], inc=sDV)
                mk_bns = sDV.n
                emit("vector", lambda e, k=k: e.bn_aggr(
                    out=stvb[k][:, 0:2], in_=st6b[:, :]),
                    waits=[(sDV.h, mk_bns)], inc=sDV)
                mk_bn = sDV.n
                emit("scalar", lambda e, k=k: e.activation(
                    stvb[k][:, 2:3], stvb[k][:, 1:2], AF.Ln, bias=epst[:, 0:1]),
                    waits=[(sDV.h, mk_bn)], inc=sAC)
                mk_ln1 = sAC.n
                emit("scalar", lambda e, k=k: e.activation(
                    stvb[k][:, 2:3], stvb[k][:, 2:3], AF.Exp, scale=-0.5),
                    waits=[(sAC.h, mk_ln1)], inc=sAC)
                mk_rstd = sAC.n
                emit("vector", lambda e, k=k: e.tensor_scalar(
                    out=stvb[k][:, 3:4], in0=stvb[k][:, 0:1], scalar1=stvb[k][:, 2:3],
                    scalar2=-1.0, op0=ALU.mult, op1=ALU.mult),
                    waits=[(sAC.h, mk_rstd), (sDV.h, mk_bn)], inc=sDV)
                emit("vector", lambda e, k=k: e.tensor_mul(
                    stvb[k][:, 6:7], stvb[k][:, 2:3], stvb[k][:, 2:3]),
                    waits=[(sAC.h, mk_rstd)], inc=sDV)
                mk_t6 = sDV.n
                emit("vector", lambda e, k=k: e.tensor_mul(
                    stvb[k][:, 6:7], stvb[k][:, 6:7], stvb[k][:, 1:2]),
                    waits=[(sDV.h, mk_t6)], inc=sDV)
                mk_q = sDV.n
                emit("scalar", lambda e, k=k: e.activation(
                    stvb[k][:, 4:5], stvb[k][:, 6:7], AF.Ln, bias=epst[:, 1:2]),
                    waits=[(sDV.h, mk_q)], inc=sAC)
                mk_ln2 = sAC.n
                emit("scalar", lambda e, k=k: e.activation(
                    stvb[k][:, 4:5], stvb[k][:, 4:5], AF.Exp, scale=-0.5),
                    waits=[(sAC.h, mk_ln2)], inc=sAC)
                mk_r2 = sAC.n
                emit("vector", lambda e, k=k: e.tensor_scalar(
                    out=stvb[k][:, 5:6], in0=stvb[k][:, 4:5], scalar1=10.0,
                    scalar2=None, op0=ALU.mult), waits=[(sAC.h, mk_r2)], inc=sDV)
                mk_r10 = sDV.n
                mk_apl[o] = emit("scalar", lambda e, k=k: e.activation(
                    xnt2[k][:, :], xqst[k][:, :], AF.Identity,
                    bias=stvb[k][:, 3:4], scale=stvb[k][:, 2:3]),
                    waits=[mk_xnqst.get(o - 2),
                           (sPE.h, mk_trlast.get(o - 2, 0)),
                           (sAC.h, mk_r2), (sDV.h, mk_r10)], inc=sAC)
                mk_xnqst[o] = dma("gpsimd",
                                  xnq_s[o * 128:(o + 1) * 128, :],
                                  xnt2[k][:, :],
                                  waits=[(sAC.h, mk_apl[o])])
                for cg in range(4):
                    bank = cg % 2
                    def trg(e, k=k, cg=cg, bank=bank):
                        i = None
                        for c in range(8):
                            cc = cg * 8 + c
                            i = e.transpose(pbb[bank][:, c * 128:(c + 1) * 128],
                                            xnt2[k][:, cc * 128:(cc + 1) * 128],
                                            ident[:, :])
                        return i
                    trw = [(sAC.h, mk_apl[o]), (sAC.h, mk_evac.get((o, cg - 2), 0))]
                    if cg < 2:
                        trw.append((sAC.h, mk_evac.get((o - 1, cg + 2), 0)))
                    emit("tensor", trg, waits=trw, inc=sPE)
                    mk_tr = sPE.n
                    mk_evac[(o, cg)] = emit(
                        "scalar", lambda e, k=k, cg=cg, bank=bank: e.activation(
                            xnfm[k][:, cg * 1024:(cg + 1) * 1024],
                            pbb[bank][:, 0:1024], AF.Identity),
                        waits=[(sPE.h, mk_tr), mk_qdma.get(o - 2)],
                        inc=sAC)
                    def lgt(e, k=k, cg=cg):
                        i = None
                        for c in range(8):
                            cc = cg * 8 + c
                            i = e.matmul(pb[2][:, 0:24],
                                         xnfm[k][:, cc * 128:(cc + 1) * 128],
                                         wcat_fm[:, cc * 24:(cc + 1) * 24],
                                         start=(cc == 0), stop=(cc == 31),
                                         skip_group_check=True)
                        return i
                    lw = [(sAC.h, mk_evac[(o, cg)])]
                    if cg == 0:
                        lw.append((sDV.h, mk_logit_evac.get(o - 1, 0)))
                    emit("tensor", lgt, waits=lw, inc=sPE)
                mk_trlast[o] = sPE.n
                mk_lg = sPE.n
                mk_qdma[o] = dma("gpsimd",
                                 qfm_s[:, o, :, :].rearrange("c d t -> d c t"),
                                 xnfm[k][:, :].rearrange("d (c t) -> d c t", c=32),
                                 waits=[(sAC.h, mk_evac[(o, 3)])])
                emit("vector", lambda e, o=o, k=k: e.scalar_tensor_tensor(
                    out=Z[:, o * 24:o * 24 + 16], in0=pb[2][:, 0:16],
                    scalar=stvb[k][:, 5:6], in1=base10[:, :],
                    op0=ALU.mult, op1=ALU.add),
                    waits=[(sPE.h, mk_lg), (sDV.h, mk_r10)], inc=sDV)
                mk_logit_evac[o] = emit("vector", lambda e, o=o, k=k: e.scalar_tensor_tensor(
                    out=Z[:, o * 24 + 16:o * 24 + 24], in0=pb[2][:, 16:24],
                    scalar=stvb[k][:, 4:5], in1=baseA[:, 16:24],
                    op0=ALU.mult, op1=ALU.add), inc=sDV)
            barrier()

            def zview(ch, r_=None, c_=None):
                v = Z[:, ch * 192:(ch + 1) * 192].rearrange(
                    "p (o i) -> p o i", i=24)[:, :, 0:16].rearrange(
                    "p o (r c) -> p o r c", c=4)
                if r_ is not None:
                    v = v[:, :, r_:r_ + 1, :].squeeze(2)
                if c_ is not None:
                    v = v[:, :, :, c_:c_ + 1].squeeze(3)
                return v

            def chviews(ch):
                tvv = tsk[:, ch * 128:(ch + 1) * 128].rearrange(
                    "p (o r c) -> p o r c", r=4, c=4)
                evv = esk[:, ch * 128:(ch + 1) * 128].rearrange(
                    "p (o r c) -> p o r c", r=4, c=4)
                nuv = nu[:, ch * 32:(ch + 1) * 32].rearrange("p (o r) -> p o r", r=4)
                nvv = nv[:, ch * 32:(ch + 1) * 32].rearrange("p (o c) -> p o c", c=4)
                return tvv, evv, nuv, nvv

            emit("vector", lambda e: e.memset(nv[:, :], 0.0), inc=sDV)
            ac_prev = {0: 0, 1: 0}
            dv_prev = {0: sDV.n, 1: sDV.n}
            for it in range(0 if 'p15b' in SKIP else 50):
                for ch in range(2):
                    tvv, evv, nuv, nvv = chviews(ch)
                    for r_ in range(4):
                        emit("vector", lambda e, ch=ch, r_=r_, tvv=tvv, nvv=nvv:
                             e.tensor_tensor(out=tvv[:, :, r_:r_ + 1, :].squeeze(2),
                                             in0=zview(ch, r_=r_), in1=nvv[:, :, :],
                                             op=ALU.subtract),
                             waits=[(sAC.h, ac_prev[ch])], inc=sDV)
                    mk_t = sDV.n
                    emit("scalar", lambda e, tvv=tvv, evv=evv: e.activation(
                        evv[:, :, :, :], tvv[:, :, :, :], AF.Exp),
                        waits=[(sDV.h, mk_t)], inc=sAC)
                    mk_e = sAC.n
                    emit("vector", lambda e, ch=ch, evv=evv: e.reduce_sum(
                        out=ssk[:, ch * 32:(ch + 1) * 32], in_=evv[:, :, :, :],
                        axis=AX.X), waits=[(sAC.h, mk_e)], inc=sDV)
                    mk_s = sDV.n
                    emit("scalar", lambda e, ch=ch: e.activation(
                        nu[:, ch * 32:(ch + 1) * 32],
                        ssk[:, ch * 32:(ch + 1) * 32], AF.Ln),
                        waits=[(sDV.h, mk_s)], inc=sAC)
                    ac_prev[ch] = sAC.n
                for ch in range(2):
                    tvv, evv, nuv, nvv = chviews(ch)
                    for c_ in range(4):
                        emit("vector", lambda e, ch=ch, c_=c_, tvv=tvv, nuv=nuv:
                             e.tensor_tensor(
                                 out=tvv[:, :, :, c_:c_ + 1].squeeze(3),
                                 in0=zview(ch, c_=c_), in1=nuv[:, :, :],
                                 op=ALU.subtract),
                             waits=[(sAC.h, ac_prev[ch])], inc=sDV)
                    mk_t2 = sDV.n
                    emit("scalar", lambda e, tvv=tvv, evv=evv: e.activation(
                        evv[:, :, :, :], tvv[:, :, :, :], AF.Exp),
                        waits=[(sDV.h, mk_t2)], inc=sAC)
                    mk_e2 = sAC.n
                    emit("vector", lambda e, ch=ch, evv=evv: e.reduce_sum(
                        out=ssk[:, ch * 32:(ch + 1) * 32],
                        in_=evv[:, :, :, :].transpose([0, 1, 3, 2]), axis=AX.X),
                        waits=[(sAC.h, mk_e2)], inc=sDV)
                    mk_s2 = sDV.n
                    emit("scalar", lambda e, ch=ch: e.activation(
                        nv[:, ch * 32:(ch + 1) * 32],
                        ssk[:, ch * 32:(ch + 1) * 32], AF.Ln),
                        waits=[(sDV.h, mk_s2)], inc=sAC)
                    ac_prev[ch] = sAC.n
            for ch in range(2):
                tvv, evv, nuv, nvv = chviews(ch)
                hv = Hres[:, ch * 128:(ch + 1) * 128].rearrange(
                    "p (o r c) -> p o r c", r=4, c=4)
                for c_ in range(4):
                    emit("vector", lambda e, ch=ch, c_=c_, tvv=tvv, nuv=nuv:
                         e.tensor_tensor(out=tvv[:, :, :, c_:c_ + 1].squeeze(3),
                                         in0=zview(ch, c_=c_), in1=nuv[:, :, :],
                                         op=ALU.subtract),
                         waits=[(sAC.h, ac_prev[ch])], inc=sDV)
                mk_g1 = sDV.n
                for r_ in range(4):
                    emit("vector", lambda e, r_=r_, tvv=tvv, nvv=nvv:
                         e.tensor_tensor(out=tvv[:, :, r_:r_ + 1, :].squeeze(2),
                                         in0=tvv[:, :, r_:r_ + 1, :].squeeze(2),
                                         in1=nvv[:, :, :], op=ALU.subtract),
                         waits=[(sDV.h, mk_g1)], inc=sDV)
                mk_t = sDV.n
                emit("scalar", lambda e, hv=hv, tvv=tvv: e.activation(
                    hv[:, :, :, :], tvv[:, :, :, :], AF.Exp),
                    waits=[(sDV.h, mk_t)], inc=sAC)
                mk_h = sAC.n
                emit("vector", lambda e, ch=ch: e.tensor_scalar(
                    out=Hres[:, ch * 128:(ch + 1) * 128],
                    in0=Hres[:, ch * 128:(ch + 1) * 128],
                    scalar1=gomg[:, 0:1], scalar2=None, op0=ALU.mult),
                    waits=[(sAC.h, mk_h)], inc=sDV)
                mk_hg = sDV.n
                emit("vector", lambda e, ch=ch: e.tensor_scalar(
                    out=Hres[:, ch * 128:(ch + 1) * 128].rearrange(
                        "p (o i) -> p o i", i=16)[:, :, 0:16:5],
                    in0=Hres[:, ch * 128:(ch + 1) * 128].rearrange(
                        "p (o i) -> p o i", i=16)[:, :, 0:16:5],
                    scalar1=gomg[:, 1:2], scalar2=None, op0=ALU.add),
                    waits=[(sDV.h, mk_hg)], inc=sDV)
            zpre = Z[:, :].rearrange("p (o i) -> p o i", i=24)[:, :, 16:20]
            zpost = Z[:, :].rearrange("p (o i) -> p o i", i=24)[:, :, 20:24]
            for zsrc, hdst in ((zpre, Hpre), (zpost, Hpost)):
                emit("scalar", lambda e, zsrc=zsrc, hdst=hdst: e.activation(
                    hdst[:, :].rearrange("p (o c) -> p o c", c=4), zsrc, AF.Exp),
                    inc=sAC)
                mk_e = sAC.n
                emit("vector", lambda e, hdst=hdst: e.reduce_sum(
                    out=ssk[:, 0:16],
                    in_=hdst[:, :].rearrange("p (o c) -> p o c", c=4), axis=AX.X),
                    waits=[(sAC.h, mk_e)], inc=sDV)
                mk_rs = sDV.n
                emit("vector", lambda e: e.reciprocal(
                    out=ssk[:, 16:32], in_=ssk[:, 0:16]),
                    waits=[(sDV.h, mk_rs)], inc=sDV)
                mk_rc = sDV.n
                for c_ in range(4):
                    emit("vector", lambda e, hdst=hdst, c_=c_: e.tensor_tensor(
                        out=hdst[:, :].rearrange(
                            "p (o c) -> p o c", c=4)[:, :, c_:c_ + 1].squeeze(2),
                        in0=hdst[:, :].rearrange(
                            "p (o c) -> p o c", c=4)[:, :, c_:c_ + 1].squeeze(2),
                        in1=ssk[:, 16:32], op=ALU.mult),
                        waits=[(sDV.h, mk_rc)], inc=sDV)
            barrier()

            mk_xp = {}
            mk_trx = {}
            _OB_C = 0 if 'p15c' in SKIP else OB
            mk_evx = {}
            mk_ymm = {}
            mk_ye = {}
            mk_yst = {}
            for o in range(_OB_C):
                k = o % 2
                mld = dma("gpsimd", xnt2[k][:, :],
                          xnq_s[o * 128:(o + 1) * 128, :],
                          waits=[(sDV.h, mk_xp.get(o - 2, 0))])
                emit("vector", lambda e, o=o, k=k: e.tensor_scalar(
                    out=acc[0][:, :], in0=xnt2[k][:, 0:1024],
                    scalar1=Hpre[:, o * 4:o * 4 + 1], scalar2=None, op0=ALU.mult),
                    waits=[mld], inc=sDV)
                emit("vector", lambda e, o=o, k=k: e.scalar_tensor_tensor(
                    out=acc[1][:, :], in0=xnt2[k][:, 1024:2048],
                    scalar=Hpre[:, o * 4 + 1:o * 4 + 2], in1=acc[0][:, :],
                    op0=ALU.mult, op1=ALU.add), inc=sDV)
                emit("vector", lambda e, o=o, k=k: e.scalar_tensor_tensor(
                    out=acc[0][:, :], in0=xnt2[k][:, 2048:3072],
                    scalar=Hpre[:, o * 4 + 2:o * 4 + 3], in1=acc[1][:, :],
                    op0=ALU.mult, op1=ALU.add), inc=sDV)
                emit("vector", lambda e, o=o, k=k: e.scalar_tensor_tensor(
                    out=acc[1][:, :], in0=xnt2[k][:, 3072:4096],
                    scalar=Hpre[:, o * 4 + 3:o * 4 + 4], in1=acc[0][:, :],
                    op0=ALU.mult, op1=ALU.add), inc=sDV)
                emit("vector", lambda e, k=k: e.tensor_copy(
                    out=xpb[k][:, :], in_=acc[1][:, :]),
                    waits=[(sPE.h, mk_trx.get(o - 2, 0))], inc=sDV)
                mk_xp[o] = sDV.n
                def trxp(e, k=k):
                    i = None
                    for c in range(8):
                        i = e.transpose(pbb[0][:, c * 128:(c + 1) * 128],
                                        xpb[k][:, c * 128:(c + 1) * 128],
                                        ident[:, :])
                    return i
                emit("tensor", trxp,
                     waits=[(sDV.h, mk_xp[o]), (sAC.h, mk_evx.get(o - 1, 0))],
                     inc=sPE)
                mk_trx[o] = sPE.n
                mk_evx[o] = emit("scalar", lambda e, k=k: e.activation(
                    xpfm[k][:, :], pbb[0][:, 0:1024], AF.Identity),
                    waits=[(sPE.h, mk_trx[o]), (sPE.h, mk_ymm.get(o - 2, 0))],
                    inc=sAC)
                def ymm(e, k=k):
                    i = None
                    for half in range(2):
                        for c in range(8):
                            i = e.matmul(pb[3 + half][:, :],
                                         xpfm[k][:, c * 128:(c + 1) * 128],
                                         wb_fm[:, c * 1024 + half * 512:
                                               c * 1024 + (half + 1) * 512],
                                         start=(c == 0), stop=(c == 7),
                                         skip_group_check=True)
                    return i
                emit("tensor", ymm,
                     waits=[(sAC.h, mk_evx[o]), (sDV.h, mk_ye.get(o - 1, 0)),
                            (sDV.h, mk_wb[7])], inc=sPE)
                mk_ymm[o] = sPE.n
                for half in range(2):
                    emit("vector", lambda e, k=k, half=half: e.tensor_copy(
                        out=yst[k][:, half * 512:(half + 1) * 512],
                        in_=pb[3 + half][:, :]),
                        waits=[(sPE.h, mk_ymm[o]), mk_yst.get(o - 2)],
                        inc=sDV)
                mk_ye[o] = sDV.n
                mk_yst[o] = dma("gpsimd", y_s[o * 128:(o + 1) * 128, :],
                                yst[k][:, :], waits=[(sDV.h, mk_ye[o])])
        barrier()

        with contextlib.ExitStack() as p2:
            PE2 = p2.enter_context
            K = PE2(nc.sbuf_tensor("K", [128, 32 * 1024], BF16))
            V = PE2(nc.sbuf_tensor("V", [128, 32 * 1024], BF16))
            Qp = [PE2(nc.sbuf_tensor(f"Qp{i}", [128, 8 * 384], BF16)) for i in range(2)]
            attnT = [PE2(nc.sbuf_tensor(f"attnT{i}", [128, 384], BF16)) for i in range(2)]
            xn4 = [PE2(nc.sbuf_tensor(f"xn4_{i}", [128, 4096], BF16)) for i in range(2)]
            xsg4 = [PE2(nc.sbuf_tensor(f"xsg4_{i}", [128, 4096], BF16)) for i in range(2)]
            sgn_h = PE2(nc.sbuf_tensor("sgn_h", [128, 1024], BF16))
            sgv_h = PE2(nc.sbuf_tensor("sgv_h", [128, 1024], BF16))
            sst = [PE2(nc.sbuf_tensor(f"sst{i}", [128, 1024], BF16)) for i in range(2)]

            emit("scalar", lambda e: e.activation(
                warm[:, 0:1], epst[:, 0:1], AF.Sigmoid), inc=sAC)
            for h in range(0 if 'p2' in SKIP else NH):
                barrier()
                m_sg = dma("sync", sgn_h[:, :],
                           sgnqk_s[h:h + 1, :].broadcast_to((128, HD)))
                m_sv = dma("sync", sgv_h[:, :],
                           sgnv_s[h:h + 1, :].broadcast_to((128, HD)))

                mk_tt = {}
                mk_ktr = {}
                mk_kev = {}
                for gidx in range(8):
                    k = gidx % 2
                    mld = dma("sync",
                              xn4[k][:, :].rearrange("p (f d) -> p f d", f=4),
                              xn_s[gidx * 512:(gidx + 1) * 512,
                                   h * 1024:(h + 1) * 1024].rearrange(
                                       "(f p) d -> p f d", p=128),
                              waits=[m_sv, (sDV.h, mk_tt.get(gidx - 2, 0)),
                                     (sPE.h, mk_ktr.get((gidx - 2, 3), 0))])
                    def ttg(e, k=k):
                        i = None
                        for f in range(4):
                            i = e.tensor_tensor(
                                out=xsg4[k][:, f * 1024:(f + 1) * 1024],
                                in0=xn4[k][:, f * 1024:(f + 1) * 1024],
                                in1=sgn_h[:, :], op=ALU.mult)
                        return i
                    def ttv(e, gidx=gidx, k=k):
                        i = None
                        for f in range(4):
                            sb = gidx * 4 + f
                            i = e.tensor_tensor(
                                out=V[:, sb * 1024:(sb + 1) * 1024],
                                in0=xn4[k][:, f * 1024:(f + 1) * 1024],
                                in1=sgv_h[:, :], op=ALU.mult)
                        return i
                    emit("vector", ttg, waits=[mld, m_sg], inc=sDV)
                    emit("vector", ttv, waits=[m_sv], inc=sDV)
                    mk_tt[gidx] = sDV.n
                    for f in range(4):
                        sb = gidx * 4 + f
                        bank = sb % 2
                        def ktr(e, k=k, f=f, bank=bank):
                            i = None
                            for c in range(8):
                                i = e.transpose(
                                    pbb[bank][:, c * 128:(c + 1) * 128],
                                    xsg4[k][:, (f * 8 + c) * 128:
                                            (f * 8 + c + 1) * 128],
                                    ident[:, :])
                            return i
                        emit("tensor", ktr,
                             waits=[(sDV.h, mk_tt[gidx]),
                                    (sDV.h, mk_kev.get(sb - 2, 0))], inc=sPE)
                        mk_ktr[(gidx, f)] = sPE.n
                        mk_kev[sb] = emit(
                            "vector", lambda e, sb=sb, bank=bank: e.tensor_copy(
                                out=K[:, sb * 1024:(sb + 1) * 1024],
                                in_=pbb[bank][:, 0:1024]),
                            waits=[(sPE.h, mk_ktr[(gidx, f)])], inc=sDV)

                pb_guard = {0: (sDV.h, mk_kev[30]), 1: (sDV.h, mk_kev[31])}
                pbav_free = {i: None for i in range(2, 8)}
                attn_user = {0: 0, 1: 0}
                sst_store = {0: None, 1: None}
                mk_qld = {}
                mk_scores_done = {}
                for pi, pack in enumerate(PACKS[:NPK]):
                    sz = len(pack)
                    Np = sz * 128
                    nsb = n_sb_of(pack)
                    relbase = 2 * pack[0]
                    kq = pi % 2
                    for j, o_ in enumerate(pack):
                        mk_qld[pi] = dma(
                            "sync",
                            Qp[kq][:, 0:8 * sz * 128].rearrange(
                                "p (c o t) -> p c o t", c=8, o=sz)[
                                :, :, j:j + 1, :].squeeze(2),
                            qfm_s[8 * h:8 * h + 8, o_, :, :].rearrange(
                                "c d t -> d c t"),
                            waits=[(sPE.h, mk_scores_done.get(pi - 2, 0))])

                    pend_av = []
                    mk_attn_ready = {}
                    mk_av = {}
                    for sb in range(0 if NOSC else nsb):
                        bank = sb % 2
                        def smm(e, sb=sb, bank=bank, kq=kq, Np=Np, sz=sz):
                            i = None
                            for c in range(8):
                                i = e.matmul(
                                    pb[bank][:, 0:Np],
                                    K[:, sb * 1024 + c * 128:
                                      sb * 1024 + (c + 1) * 128],
                                    Qp[kq][:, c * sz * 128:(c + 1) * sz * 128],
                                    start=(c == 0), stop=(c == 7),
                                    skip_group_check=True)
                            return i
                        emit("tensor", smm,
                             waits=[mk_qld[pi], (sDV.h, mk_kev[sb]),
                                    pb_guard[bank]], inc=sPE)
                        mk_sc = sPE.n
                        if sb == nsb - 1:
                            mk_scores_done[pi] = mk_sc
                        ka = sb % 2
                        if NOSIG:
                            continue
                        mk_sig = emit(
                            "scalar", lambda e, bank=bank, ka=ka, Np=Np, h=h:
                            e.activation(attnT[ka][:, 0:Np], pb[bank][:, 0:Np],
                                         AF.Sigmoid, scale=sigsc[:, h:h + 1]),
                            waits=[(sPE.h, mk_sc), (sPE.h, attn_user[ka])],
                            inc=sAC)
                        pb_guard[bank] = (sAC.h, mk_sig)
                        rel = sb - relbase
                        if rel >= 0 and not NOMASK:
                            mview = (m1[:, rel * 128:(rel + 1) * 128] if sz == 1
                                     else m3[:, rel * 384:(rel + 1) * 384])
                            mk_msk = emit(
                                "vector", lambda e, ka=ka, Np=Np, mview=mview:
                                e.tensor_tensor(out=attnT[ka][:, 0:Np],
                                                in0=attnT[ka][:, 0:Np],
                                                in1=mview, op=ALU.mult),
                                waits=[(sAC.h, mk_sig)], inc=sDV)
                            mk_attn_ready[sb] = (sDV.h, mk_msk)
                        else:
                            mk_attn_ready[sb] = (sAC.h, mk_sig)

                        def avmm(e, sb=sb, ka=ka, pack=pack):
                            i = None
                            for j, o in enumerate(pack):
                                if sb < n_av_of(o):
                                    for half in range(2):
                                        i = e.matmul(
                                            pb[2 + 2 * j + half][:, :],
                                            attnT[ka][:, j * 128:(j + 1) * 128],
                                            V[:, sb * 1024 + half * 512:
                                              sb * 1024 + (half + 1) * 512],
                                            start=(sb == 0),
                                            stop=(sb == n_av_of(o) - 1),
                                            skip_group_check=True)
                            return i
                        if NOAV:
                            continue
                        avw = [mk_attn_ready[sb]]
                        if sb == 0:
                            avw += [pbav_free[i] for i in range(2, 2 + 2 * sz)]
                        pend_av.append((avmm, avw, sb))
                        if len(pend_av) >= 2:
                            fn_, w_, sb_ = pend_av.pop(0)
                            emit("tensor", fn_, waits=w_, inc=sPE)
                            mk_av[sb_] = sPE.n
                            attn_user[sb_ % 2] = sPE.n
                    while pend_av:
                        fn_, w_, sb_ = pend_av.pop(0)
                        emit("tensor", fn_, waits=w_, inc=sPE)
                        mk_av[sb_] = sPE.n
                        attn_user[sb_ % 2] = sPE.n

                    for j, o in enumerate(pack):
                        if NOAV or NOSIG or NOSC:
                            break
                        ks = j % 2
                        last = mk_av[n_av_of(o) - 1]
                        mk_eh = emit("scalar", lambda e, j=j, ks=ks, h=h:
                                     e.activation(sst[ks][:, 0:512],
                                                  pb[2 + 2 * j][:, :], AF.Identity,
                                                  scale=avsc[:, h:h + 1]),
                                     waits=[(sPE.h, last),
                                            sst_store[ks]], inc=sAC)
                        pbav_free[2 + 2 * j] = (sAC.h, mk_eh)
                        mk_ev = emit("vector", lambda e, j=j, ks=ks, h=h:
                                     e.tensor_scalar(
                                         out=sst[ks][:, 512:1024],
                                         in0=pb[2 + 2 * j + 1][:, :],
                                         scalar1=avsc[:, h:h + 1], scalar2=None,
                                         op0=ALU.mult),
                                     waits=[(sPE.h, last),
                                            sst_store[ks]], inc=sDV)
                        pbav_free[2 + 2 * j + 1] = (sDV.h, mk_ev)
                        sst_store[ks] = dma(
                            "gpsimd", s_s[o, :, h, :], sst[ks][:, :],
                            waits=[(sAC.h, mk_eh), (sDV.h, mk_ev)])
        barrier()

        with contextlib.ExitStack() as p3:
            PE3 = p3.enter_context
            xq3 = [PE3(nc.sbuf_tensor(f"xq3_{i}", [128, D], F32)) for i in range(2)]
            s3 = [PE3(nc.sbuf_tensor(f"s3_{i}", [128, D], BF16)) for i in range(2)]
            xn3 = [PE3(nc.sbuf_tensor(f"xn3_{i}", [128, D], BF16)) for i in range(2)]
            y3 = [PE3(nc.sbuf_tensor(f"y3_{i}", [128, 1024], F32)) for i in range(2)]
            yb3 = [PE3(nc.sbuf_tensor(f"yb3_{i}", [128, 1024], BF16)) for i in range(2)]
            dg = [PE3(nc.sbuf_tensor(f"dg{i}", [128, 20 * 128], BF16)) for i in range(2)]
            gb = [PE3(nc.sbuf_tensor(f"gb{i}", [128, D], BF16)) for i in range(2)]
            outst = [PE3(nc.sbuf_tensor(f"outst{i}", [128, D], F32)) for i in range(2)]

            emit("scalar", lambda e: e.activation(
                warm[:, 0:1], epst[:, 0:1], AF.Silu), inc=sAC)
            mk_out = {}
            mk_ffn = {}
            mk_fin_dv = {}
            mk_fin_po = {}
            mk_sg = {}
            mk_silu = {}
            for o in range(0 if 'p3' in SKIP else OB):
                k = o % 2
                m_xq = dma("sync", xq3[k][:, :],
                           xq[o * 128:(o + 1) * 128, :],
                           waits=[(sDV.h, mk_fin_dv.get(o - 2, 0)),
                                  (sPO.h, mk_fin_po.get(o - 2, 0))])
                m_s = dma("sync", s3[k][:, :],
                          s_s[o, :, :, :].rearrange("p h d -> p (h d)"),
                          waits=[(sDV.h, mk_fin_dv.get(o - 2, 0)),
                                 (sPO.h, mk_fin_po.get(o - 2, 0))])
                m_xn = dma("gpsimd", xn3[k][:, :],
                           xnq_s[o * 128:(o + 1) * 128, :],
                           waits=[(sPE.h, mk_ffn.get(o - 2, 0))])
                m_y = dma("gpsimd", y3[k][:, :],
                          y_s[o * 128:(o + 1) * 128, :],
                          waits=[(sDV.h, mk_fin_dv.get(o - 2, 0))])
                def dgb(e, o=o, k=k):
                    for t in range(16):
                        e.tensor_scalar(out=dg[k][:, t * 128:(t + 1) * 128],
                                        in0=ident[:, :],
                                        scalar1=Hres[:, o * 16 + t:o * 16 + t + 1],
                                        scalar2=None, op0=ALU.mult)
                    i = None
                    for t in range(4):
                        i = e.tensor_scalar(
                            out=dg[k][:, (16 + t) * 128:(17 + t) * 128],
                            in0=ident[:, :],
                            scalar1=Hpost[:, o * 4 + t:o * 4 + t + 1],
                            scalar2=None, op0=ALU.mult)
                    return i
                emit("vector", dgb, waits=[(sPE.h, mk_ffn.get(o - 2, 0))], inc=sDV)
                emit("vector", lambda e, k=k: e.tensor_copy(
                    out=yb3[k][:, :], in_=y3[k][:, :]),
                    waits=[m_y, (sPE.h, mk_ffn.get(o - 2, 0))], inc=sDV)
                mk_dg = sDV.n
                for i4 in range(4):
                    for half in range(2):
                        bank = i4 * 2 + half
                        def fmm(e, i4=i4, half=half, bank=bank, k=k):
                            for j in range(4):
                                e.matmul(pb[bank][:, :],
                                         dg[k][:, (i4 * 4 + j) * 128:
                                               (i4 * 4 + j + 1) * 128],
                                         xn3[k][:, j * 1024 + half * 512:
                                                j * 1024 + (half + 1) * 512],
                                         start=(j == 0), stop=False,
                                         skip_group_check=True)
                            return e.matmul(pb[bank][:, :],
                                            dg[k][:, (16 + i4) * 128:
                                                  (17 + i4) * 128],
                                            yb3[k][:, half * 512:(half + 1) * 512],
                                            start=False, stop=True,
                                            skip_group_check=True)
                        emit("tensor", fmm,
                             waits=[(sDV.h, mk_dg), m_xn,
                                    (sAC.h, mk_silu.get((o - 1, i4, half), 0))],
                             inc=sPE)
                        mk_f = sPE.n
                        mk_silu[(o, i4, half)] = emit(
                            "scalar", lambda e, i4=i4, half=half, bank=bank, k=k:
                            e.activation(
                                gb[k][:, i4 * 1024 + half * 512:
                                      i4 * 1024 + (half + 1) * 512],
                                pb[bank][:, :], AF.Silu),
                            waits=[(sPE.h, mk_f), (sDV.h, mk_sg.get(o - 2, 0))],
                            inc=sAC)
                mk_ffn[o] = sPE.n
                mk_g = sAC.n
                if KDBG:
                    dma("gpsimd", dbg_g[o * 128:(o + 1) * 128, :],
                        gb[k][:, :], waits=[(sAC.h, mk_g)])
                emit("vector", lambda e, k=k: e.tensor_tensor(
                    out=s3[k][:, :], in0=s3[k][:, :], in1=gb[k][:, :],
                    op=ALU.mult),
                    waits=[(sAC.h, mk_g), m_s], inc=sDV)
                mk_sg[o] = sDV.n
                if NOPOOL:
                    emit("vector", lambda e, k=k: e.tensor_tensor(
                        out=outst[k][:, :], in0=xq3[k][:, :],
                        in1=s3[k][:, :], op=ALU.add),
                        waits=[m_xq, mk_out.get(o - 2)],
                        inc=sDV)
                    mk_fin_dv[o] = sDV.n
                    mk_fin_po[o] = 0
                else:
                    emit("vector", lambda e, k=k: e.tensor_tensor(
                        out=outst[k][:, 0:2048], in0=xq3[k][:, 0:2048],
                        in1=s3[k][:, 0:2048], op=ALU.add),
                        waits=[m_xq, mk_out.get(o - 2)], inc=sDV)
                    mk_fin_dv[o] = sDV.n
                    emit("gpsimd", lambda e, k=k: e.tensor_tensor(
                        out=outst[k][:, 2048:4096], in0=xq3[k][:, 2048:4096],
                        in1=s3[k][:, 2048:4096], op=ALU.add),
                        waits=[(sDV.h, mk_sg[o]), m_xq,
                               mk_out.get(o - 2)], inc=sPO)
                    mk_fin_po[o] = sPO.n
                mk_out[o] = dma("sync", out[o * 128:(o + 1) * 128, :],
                                outst[k][:, :],
                                waits=[(sDV.h, mk_fin_dv[o]),
                                       (sPO.h, mk_fin_po[o])])

            if KDBG:
                barrier()
                dma("sync", dbg_xn[:, :], xn_s[:, :])
                dma("sync", dbg_xnq[:, :], xnq_s[:, :])
                dma("sync", dbg_z[:, :], Z[:, :])
                dma("sync", dbg_h[:, 0:OB * 16], Hres[:, :])
                dma("sync", dbg_h[:, OB * 16:OB * 20], Hpre[:, :])
                dma("sync", dbg_h[:, OB * 20:OB * 24], Hpost[:, :])
                dma("sync", dbg_y[:, :], y_s[:, :])
                dma("sync", dbg_ss[:, :, :, :], s_s[:, :, :, :])
                dma("sync", dbg_q[:, :, :, :], qfm_s[:, :, :, :])
            emit("sync", lambda e: None,
                 waits=[(s.h, s.n) for q in (qS, qG) for s in q.slots])

        block = GE(nc.Block())
        for eng_name in ENGS:
            def body(e, eng_name=eng_name):
                done = {}
                for waits, fn, tgt in prog[eng_name]:
                    for (hh, v) in waits:
                        key = id(hh)
                        if done.get(key, 0) < v:
                            e.wait_ge(hh, v)
                            done[key] = v
                    inst = fn(e)
                    if tgt is not None:
                        inst.then_inc(tgt[0], tgt[1])
            getattr(block, eng_name)(body)
    return nc


def _host_masks(r):
    s = np.arange(128)[:, None]
    t = np.arange(128)[None, :]
    m1 = np.zeros((2, 128, 128), np.float32)
    for rel in range(2):
        m1[rel] = ((r - rel) * 128 + t - s >= 0).astype(np.float32)
    m3 = np.zeros((6, 128, 384), np.float32)
    for rel in range(6):
        for j in range(3):
            m3[rel][:, j * 128:(j + 1) * 128] = \
                ((2 * j + r - rel) * 128 + t - s >= 0).astype(np.float32)
    return m1, m3


_NC_CACHE = {}


def kernel(L0_attn_x, L0_attn_bv_q, L0_attn_bv_k, L0_attn_bv_v,
           L0_ffn_W_res, L0_ffn_W_pre, L0_ffn_W_post, L0_ffn_H_res_base,
           L0_ffn_H_pre_base, L0_ffn_H_post_base, L0_ffn_gate, L0_ffn_W_branch):
    x = np.ascontiguousarray(np.asarray(L0_attn_x, dtype=np.float32))
    wcat = np.ascontiguousarray(np.concatenate([
        np.asarray(L0_ffn_W_res, np.float32),
        np.asarray(L0_ffn_W_pre, np.float32),
        np.asarray(L0_ffn_W_post, np.float32)], axis=0))
    basecat = np.ascontiguousarray(np.concatenate([
        np.asarray(L0_ffn_H_res_base, np.float32).reshape(16),
        np.asarray(L0_ffn_H_pre_base, np.float32).reshape(4),
        np.asarray(L0_ffn_H_post_base, np.float32).reshape(4)]).reshape(1, 24))
    gate = np.asarray(L0_ffn_gate, np.float32).reshape(1, 1)
    bvq = np.ascontiguousarray(np.asarray(L0_attn_bv_q, np.float32))
    bvk = np.ascontiguousarray(np.asarray(L0_attn_bv_k, np.float32))
    bvv = np.ascontiguousarray(np.asarray(L0_attn_bv_v, np.float32))
    wbr = np.ascontiguousarray(np.asarray(L0_ffn_W_branch, np.float32))

    if "nc" not in _NC_CACHE:
        _NC_CACHE["nc"] = build_program()
    nc = _NC_CACHE["nc"]

    in_maps = []
    for c in range(8):
        b, r = c // 2, c % 2
        xb = x[b]
        rows = np.concatenate([np.arange((2 * o + r) * 128,
                                         (2 * o + r) * 128 + 128)
                               for o in range(OB)])
        m1h, m3h = _host_masks(r)
        in_maps.append({
            "xfull": xb,
            "xq": np.ascontiguousarray(xb[rows]),
            "mask1": m1h, "mask3": m3h,
            "bvq": bvq, "bvk": bvk, "bvv": bvv,
            "wcat": wcat, "basecat": basecat, "gate": gate, "wbr": wbr,
        })
    res = run_bass_kernel_spmd(nc, in_maps, core_ids=list(range(8)))
    outf = np.empty((4, T, D), dtype=np.float32)
    for c in range(8):
        b, r = c // 2, c % 2
        o_c = np.asarray(res.results[c]["out"], dtype=np.float32)
        for o in range(OB):
            outf[b, (2 * o + r) * 128:(2 * o + r) * 128 + 128, :] = \
                o_c[o * 128:(o + 1) * 128, :]
    return outf
